# version 12
# baseline (speedup 1.0000x reference)
import numpy as np
import ml_dtypes
from contextlib import ExitStack

import concourse.bass as bass
import concourse.mybir as mybir
from concourse.bass_utils import run_bass_kernel_spmd

F32 = mybir.dt.float32
BF16 = mybir.dt.bfloat16
I32 = mybir.dt.int32
AF = mybir.ActivationFunctionType
ALU = mybir.AluOpType
AX = mybir.AxisListType

D = 4096
NT = 32
NTOWN = 16
TLOC = NT * 128
TOWN = NTOWN * 128
HK = 8
HQ = 32
NGRP = 64
EPS = 1e-5
SCL = float(1.0 / np.sqrt(128.0))
NEG = -1.0e30

ENGINES = ("sync", "gpsimd", "tensor", "vector", "scalar")


class S:
    __slots__ = ("h", "n", "name")

    def __init__(self, h, name):
        self.h = h
        self.n = 0
        self.name = name

    def rec(self):
        return (self, self.n)


def _mkop(waits, fn, incs):
    def run(eng):
        for (h, v) in waits:
            eng.wait_ge(h, v)
        ins = fn(eng)
        for (h, k) in incs:
            ins.then_inc(h, k)
    return run


class Plan:
    def __init__(self):
        self.q = {e: [] for e in ENGINES}
        self.selfsem = {}

    def op(self, e, fn, waits=(), incs=()):
        waits = list(waits)
        ss = self.selfsem.get(e)
        if ss is not None and ss.n > 0:
            waits.append((ss, ss.n))
        if ss is not None and not incs:
            incs = [(ss, 1)]
        w = tuple((s.h, int(v)) for (s, v) in waits if v > 0)
        ii = []
        for (s, k) in incs:
            s.n += k
            ii.append((s.h, k))
        self.q[e].append(_mkop(w, fn, tuple(ii)))

    def wait_all(self, e, sems):
        w = tuple((s.h, s.n) for s in sems if s.n > 0)

        def run(eng):
            for (h, v) in w:
                eng.wait_ge(h, v)
        self.q[e].append(run)


class CC:

    def __init__(self, h):
        self.h = h
        self.n = 0

    def link(self, ins):
        ins.then_inc(self.h, 1)
        self.n += 1

    def step(self, eng, ins, last=False):
        self.link(ins)

    def barrier(self, eng):
        eng.wait_ge(self.h, self.n)


def chained(eng, cc, steps):
    last = None
    for k, f in enumerate(steps):
        if k > 0:
            cc.link(last)
            cc.barrier(eng)
        last = f(eng)
    return last


class Chan:

    def __init__(self, plan, eng, sems):
        self.plan = plan
        self.eng = eng
        self.sems = sems
        self.i = 0

    def issue(self, fn, waits=()):
        s = self.sems[self.i % len(self.sems)]
        w = [(s, s.n)] + list(waits)
        self.plan.op(self.eng, fn, w, [(s, 16)])
        self.i += 1
        return (s, s.n)


def build_nc():
    nc = bass.Bass("TRN2", target_bir_lowering=False, debug=False)

    x_in = nc.dram_tensor("x_in", [TLOC, D], F32, kind="ExternalInput").ap()
    wq = nc.dram_tensor("wq", [D, D], F32, kind="ExternalInput").ap()
    wk = nc.dram_tensor("wk", [D, HK * 128], F32, kind="ExternalInput").ap()
    wv = nc.dram_tensor("wv", [D, HK * 128], F32, kind="ExternalInput").ap()
    wo = nc.dram_tensor("wo", [D, D], F32, kind="ExternalInput").ap()
    wqint = nc.dram_tensor("wqint", [D, D], I32, kind="ExternalInput").ap()
    zp = nc.dram_tensor("zp", [D, NGRP], F32, kind="ExternalInput").ap()
    qsc = nc.dram_tensor("qsc", [D, NGRP], F32, kind="ExternalInput").ap()
    fbias = nc.dram_tensor("fbias", [1, D], F32, kind="ExternalInput").ap()
    cosr = nc.dram_tensor("cosr", [64, TLOC], F32, kind="ExternalInput").ap()
    sinr = nc.dram_tensor("sinr", [64, TLOC], F32, kind="ExternalInput").ap()
    mtri = nc.dram_tensor("mtri", [128, 128], F32, kind="ExternalInput").ap()
    mpar = nc.dram_tensor("mpar", [128, 128], F32, kind="ExternalInput").ap()
    identf = nc.dram_tensor("identf", [128, 128], F32, kind="ExternalInput").ap()

    o_out = nc.dram_tensor("o_out", [TOWN, D], F32, kind="ExternalOutput").ap()

    d_wt = nc.dram_tensor("d_wt", [D, D], BF16, kind="Internal").ap()
    d_x1h = nc.dram_tensor("d_x1h", [NT, 128, D], BF16, kind="Internal").ap()
    d_x2 = nc.dram_tensor("d_x2", [TLOC, D], F32, kind="Internal").ap()
    d_x2h = nc.dram_tensor("d_x2h", [TLOC, D], BF16, kind="Internal").ap()
    d_k = nc.dram_tensor("d_k", [HK, 128, TLOC], BF16, kind="Internal").ap()
    d_v = nc.dram_tensor("d_v", [HK, 128, TLOC], BF16, kind="Internal").ap()
    d_q = nc.dram_tensor("d_q", [HQ, 128, TOWN], BF16, kind="Internal").ap()
    d_ot = nc.dram_tensor("d_ot", [D, TOWN], BF16, kind="Internal").ap()
    d_x3 = nc.dram_tensor("d_x3", [TOWN, D], F32, kind="Internal").ap()
    d_x3h = nc.dram_tensor("d_x3h", [NTOWN, 128, D], BF16, kind="Internal").ap()

    plan = Plan()

    with ExitStack() as top:
        E = top.enter_context
        pe = S(E(nc.semaphore("s_pe")), "pe")
        dv = S(E(nc.semaphore("s_dv")), "dv")
        ac = S(E(nc.semaphore("s_ac")), "ac")
        gp = S(E(nc.semaphore("s_gp")), "gp")
        lds = [S(E(nc.semaphore(f"s_ld{i}")), f"ld{i}") for i in range(6)]
        gds = [S(E(nc.semaphore(f"s_gd{i}")), f"gd{i}") for i in range(4)]
        sts = [S(E(nc.semaphore(f"s_st{i}")), f"st{i}") for i in range(4)]
        SEMS = tuple([pe, dv, ac, gp] + lds + gds + sts)

        plan.selfsem = {"vector": dv, "scalar": ac, "tensor": pe, "gpsimd": gp}
        CDV = CC(E(nc.semaphore("s_cdv")))
        CAC = CC(E(nc.semaphore("s_cac")))
        LD = Chan(plan, "sync", lds)
        GD = Chan(plan, "gpsimd", gds)
        ST = Chan(plan, "sync", sts)

        def fence():
            for e in ENGINES:
                plan.wait_all(e, SEMS)

        c_ones1 = E(nc.sbuf_tensor("c_ones1", [1, 128], BF16))
        c_onesf = E(nc.sbuf_tensor("c_onesf", [1, 128], F32))
        c_onescol = E(nc.sbuf_tensor("c_onescol", [128, 1], BF16))
        c_idb = E(nc.sbuf_tensor("c_idb", [128, 128], BF16))
        c_idf = E(nc.sbuf_tensor("c_idf", [128, 128], F32))
        c_mtri = E(nc.sbuf_tensor("c_mtri", [128, 128], F32))
        c_mpar = E(nc.sbuf_tensor("c_mpar", [128, 128], F32))

        LD.issue(lambda eng: eng.dma_start(out=c_mtri[:], in_=mtri))
        LD.issue(lambda eng: eng.dma_start(out=c_mpar[:], in_=mpar))
        LD.issue(lambda eng: eng.dma_start(out=c_idf[:], in_=identf))
        GD.issue(lambda eng: eng.dma_start(out=c_idb[:], in_=identf))
        plan.op("gpsimd", lambda eng: eng.memset(c_ones1[:], 1.0), (), [(gp, 1)])
        plan.op("gpsimd", lambda eng: eng.memset(c_onesf[:], 1.0), (), [(gp, 1)])
        plan.op("gpsimd", lambda eng: eng.memset(c_onescol[:], 1.0), (), [(gp, 1)])
        fence()

        with ExitStack() as ph:
            PE_ = ph.enter_context
            s_wq = [PE_(nc.sbuf_tensor(f"p0_wq{i}", [128, D], BF16)) for i in range(2)]
            s_zp = [PE_(nc.sbuf_tensor(f"p0_zp{i}", [128, NGRP], F32)) for i in range(2)]
            s_sc = [PE_(nc.sbuf_tensor(f"p0_sc{i}", [128, NGRP], F32)) for i in range(2)]
            s_wd = [PE_(nc.sbuf_tensor(f"p0_wd{i}", [128, D], BF16)) for i in range(2)]
            s_wt = [PE_(nc.sbuf_tensor(f"p0_wt{i}", [128, D], BF16)) for i in range(2)]
            p_tr = [PE_(nc.psum_tensor(f"p0_ptr{i}", [128, 1024], BF16)) for i in range(4)]

            r_deq = {}
            r_tr = {}
            r_ev = {}
            r_st = {}
            for nb in range(NT):
                bi = nb % 2
                w_gd = [r_deq[nb - 2]] if nb >= 2 else []
                t_wq = GD.issue(
                    lambda eng, nb=nb, bi=bi: eng.dma_start(
                        out=s_wq[bi][:], in_=wqint[nb * 128:(nb + 1) * 128, :]),
                    w_gd)
                t_zp = LD.issue(
                    lambda eng, nb=nb, bi=bi: eng.dma_start(
                        out=s_zp[bi][:], in_=zp[nb * 128:(nb + 1) * 128, :]),
                    w_gd)
                t_sc = LD.issue(
                    lambda eng, nb=nb, bi=bi: eng.dma_start(
                        out=s_sc[bi][:], in_=qsc[nb * 128:(nb + 1) * 128, :]))
                w0 = [t_wq, t_zp, t_sc]
                if nb >= 2:
                    w0.append(r_tr[(nb - 2, 3)])
                for g in range(NGRP):
                    inc = [(dv, 1)] if g == NGRP - 1 else []
                    plan.op("vector",
                            lambda eng, bi=bi, g=g: eng.tensor_scalar(
                                s_wd[bi][:, g * 64:(g + 1) * 64],
                                s_wq[bi][:, g * 64:(g + 1) * 64],
                                s_zp[bi][:, g:g + 1], s_sc[bi][:, g:g + 1],
                                ALU.subtract, ALU.mult),
                            w0 if g == 0 else (), inc)
                r_deq[nb] = dv.rec()
                for grpi in range(4):
                    w = [r_deq[nb]] if grpi == 0 else []
                    if nb >= 1:
                        w.append(r_ev[nb - 1])
                    if nb >= 2 and grpi == 0:
                        w.extend(r_st[nb - 2])
                    for j in range(8):
                        kc = grpi * 8 + j
                        inc = [(pe, 1)] if j == 7 else []
                        plan.op("tensor",
                                lambda eng, bi=bi, kc=kc, grpi=grpi, j=j: eng.transpose(
                                    p_tr[grpi][:, j * 128:(j + 1) * 128],
                                    s_wd[bi][:, kc * 128:(kc + 1) * 128], c_idb[:]),
                                w if j == 0 else (), inc)
                    r_tr[(nb, grpi)] = pe.rec()
                    plan.op("vector",
                            lambda eng, bi=bi, grpi=grpi: eng.tensor_copy(
                                s_wt[bi][:, grpi * 1024:(grpi + 1) * 1024],
                                p_tr[grpi][:, :]),
                            [r_tr[(nb, grpi)]], [(dv, 1)])
                r_ev[nb] = dv.rec()
                stoks = []
                for part in range(4):
                    dst = d_wt.rearrange("(kc p) (nb n) -> p kc nb n", p=128, n=128)[
                        :, part * 8:(part + 1) * 8, nb, :]
                    srcv = s_wt[bi][:].rearrange("p (kc n) -> p kc n", n=128)[
                        :, part * 8:(part + 1) * 8, :]
                    stoks.append(ST.issue(
                        lambda eng, dst=dst, srcv=srcv: eng.dma_start(out=dst, in_=srcv),
                        [r_ev[nb]] if part == 0 else []))
                r_st[nb] = stoks
            fence()

        def plan_ln(src_dram, n_tiles, dst_image, xh_plain=None, tag=""):
            with ExitStack() as ph:
                PE_ = ph.enter_context
                s_x = [PE_(nc.sbuf_tensor(f"ln_x{tag}{i}", [128, D], F32))
                       for i in range(3)]
                s_scr = [PE_(nc.sbuf_tensor(f"ln_scr{tag}{i}", [128, D], F32))
                         for i in range(2)]
                s_xh = [PE_(nc.sbuf_tensor(f"ln_xh{tag}{i}", [128, D], BF16))
                        for i in range(2)]
                s_st2 = [PE_(nc.sbuf_tensor(f"ln_st{tag}{i}", [128, 8], F32))
                         for i in range(2)]
                if dst_image is not None:
                    p_tr = [PE_(nc.psum_tensor(f"ln_ptr{tag}{i}", [128, 1024], BF16))
                            for i in range(4)]
                    s_wtile = [PE_(nc.sbuf_tensor(f"ln_wt{tag}{i}", [128, D], BF16))
                               for i in range(2)]
                r_norm = {}
                r_trl = {}
                r_evl = {}
                r_stl = {}
                r_fin = {}
                for t in range(n_tiles):
                    bi = t % 2
                    xb = t % 3
                    w = [r_norm[t - 3]] if t >= 3 else []
                    t_x = LD.issue(
                        lambda eng, t=t, xb=xb: eng.dma_start(
                            out=s_x[xb][:], in_=src_dram[t * 128:(t + 1) * 128, :]),
                        w)
                    wstat = [t_x]
                    if t >= 2:
                        wstat.append(r_fin[t - 2])
                    plan.op("vector",
                            lambda eng, xb=xb, bi=bi: eng.tensor_reduce(
                                s_st2[bi][:, 0:1], s_x[xb][:], AX.X, ALU.add),
                            wstat, [(dv, 1)])
                    wsq = [t_x]
                    if t >= 2:
                        wsq.append(r_norm[t - 2])
                    plan.op("scalar",
                            lambda eng, xb=xb, bi=bi: eng.activation(
                                s_scr[bi][:], s_x[xb][:], AF.Square,
                                accum_out=s_st2[bi][:, 1:2]),
                            wsq, [(ac, 1)])
                    wac = ac.rec()

                    def fin(eng, bi=bi):
                        s = s_st2[bi]
                        return chained(eng, CDV, [
                            lambda e: e.tensor_scalar(s[:, 2:3], s[:, 0:1], 1.0 / D,
                                                      None, ALU.mult),
                            lambda e: e.tensor_scalar(s[:, 3:4], s[:, 1:2], 1.0 / D,
                                                      None, ALU.mult),
                            lambda e: e.tensor_mul(s[:, 7:8], s[:, 2:3], s[:, 2:3]),
                            lambda e: e.tensor_sub(s[:, 3:4], s[:, 3:4], s[:, 7:8]),
                            lambda e: e.tensor_scalar(s[:, 3:4], s[:, 3:4], EPS,
                                                      None, ALU.add),
                        ])
                    plan.op("vector", fin, [wac], [(dv, 1)])
                    wdvf = dv.rec()

                    def rstd(eng, bi=bi):
                        s = s_st2[bi]
                        return chained(eng, CAC, [
                            lambda e: e.activation(s[:, 4:5], s[:, 3:4], AF.Ln),
                            lambda e: e.activation(s[:, 5:6], s[:, 4:5], AF.Exp,
                                                   scale=-0.5),
                        ])
                    plan.op("scalar", rstd, [wdvf], [(ac, 1)])
                    wacr = ac.rec()

                    def nbias(eng, bi=bi):
                        s = s_st2[bi]
                        return chained(eng, CDV, [
                            lambda e: e.tensor_mul(s[:, 6:7], s[:, 2:3], s[:, 5:6]),
                            lambda e: e.tensor_scalar(s[:, 6:7], s[:, 6:7], -1.0,
                                                      None, ALU.mult),
                        ])
                    plan.op("vector", nbias, [wacr], [(dv, 1)])
                    r_fin[t] = dv.rec()
                    wnorm = [r_fin[t]]
                    if t >= 2:
                        if dst_image is None:
                            wnorm.append(r_stl[t - 2])
                        else:
                            wnorm.append(r_trl[t - 2])
                    plan.op("scalar",
                            lambda eng, xb=xb, bi=bi: eng.activation(
                                s_xh[bi][:], s_x[xb][:], AF.Identity,
                                bias=s_st2[bi][:, 6:7], scale=s_st2[bi][:, 5:6]),
                            wnorm, [(ac, 1)])
                    r_norm[t] = ac.rec()
                    if dst_image is None:
                        r_stl[t] = ST.issue(
                            lambda eng, t=t, bi=bi: eng.dma_start(
                                out=xh_plain[t * 128:(t + 1) * 128, :], in_=s_xh[bi][:]),
                            [r_norm[t]])
                    else:
                        for grpi in range(4):
                            w2 = [r_norm[t]] if grpi == 0 else []
                            if t >= 1:
                                w2.append(r_evl[t - 1])
                            if t >= 2 and grpi == 0:
                                w2.append(r_stl[t - 2])
                            for j in range(8):
                                kc = grpi * 8 + j
                                inc = [(pe, 1)] if j == 7 else []
                                plan.op("tensor",
                                        lambda eng, bi=bi, kc=kc, grpi=grpi, j=j:
                                        eng.transpose(
                                            p_tr[grpi][:, j * 128:(j + 1) * 128],
                                            s_xh[bi][:, kc * 128:(kc + 1) * 128],
                                            c_idb[:]),
                                        w2 if j == 0 else (), inc)
                            tok = pe.rec()
                            if grpi == 3:
                                r_trl[t] = tok
                            plan.op("vector",
                                    lambda eng, bi=bi, grpi=grpi: eng.tensor_copy(
                                        s_wtile[bi][:, grpi * 1024:(grpi + 1) * 1024],
                                        p_tr[grpi][:, :]),
                                    [tok], [(dv, 1)])
                        r_evl[t] = dv.rec()
                        r_stl[t] = ST.issue(
                            lambda eng, t=t, bi=bi: eng.dma_start(
                                out=dst_image[t], in_=s_wtile[bi][:]),
                            [r_evl[t]])
                fence()

        def plan_ffn(xh_image, res_dram, out_dram, n_tiles, tag=""):
            with ExitStack() as ph:
                PE_ = ph.enter_context
                s_wth = PE_(nc.sbuf_tensor("ffn_wt" + tag, [128, NT * 2048], BF16))
                c_bh = PE_(nc.sbuf_tensor("ffn_bh" + tag, [1, D], BF16))
                c_bl = PE_(nc.sbuf_tensor("ffn_bl" + tag, [1, D], BF16))
                _b32s = ExitStack()
                c_b32 = _b32s.enter_context(nc.sbuf_tensor("ffn_b32" + tag, [1, D], F32))
                t_b = LD.issue(lambda eng: eng.dma_start(out=c_b32[:], in_=fbias))

                def _bsplit(eng):
                    return chained(eng, CDV, [
                        lambda e: e.tensor_copy(c_bh[:], c_b32[:]),
                        lambda e: e.tensor_sub(c_bl[:], c_b32[:], c_bh[:]),
                    ])
                plan.op("vector", _bsplit, [t_b], [(dv, 1)])
                _bias_tok = dv.rec()
                _b32s.close()
                s_xt = [PE_(nc.sbuf_tensor(f"ffn_xt{tag}{i}", [128, D], BF16))
                        for i in range(2)]
                s_res = [PE_(nc.sbuf_tensor(f"ffn_r{tag}{i}", [128, 2048], F32))
                         for i in range(2)]
                s_o = [PE_(nc.sbuf_tensor(f"ffn_o{tag}{i}", [128, 2048], F32))
                       for i in range(2)]
                p_acc = [PE_(nc.psum_tensor(f"ffn_p{tag}{i}", [128, 2048], F32))
                         for i in range(2)]

                r_mm = {}
                r_epi = {}
                r_sto = {}
                for half in range(2):
                    ww = [r_mm[n_tiles - 1]] if half == 1 else []
                    wt_toks = []
                    for part in range(4):
                        src = d_wt.rearrange("(kc p) n -> p kc n", p=128)[
                            :, part * 8:(part + 1) * 8, half * 2048:(half + 1) * 2048]
                        dstv = s_wth[:].rearrange("p (kc n) -> p kc n", n=2048)[
                            :, part * 8:(part + 1) * 8, :]
                        wt_toks.append(LD.issue(
                            lambda eng, src=src, dstv=dstv: eng.dma_start(
                                out=dstv, in_=src),
                            ww if part == 0 else []))
                    for t in range(n_tiles):
                        gi = half * n_tiles + t
                        bi = gi % 2
                        xb = gi % 2
                        wl = list(wt_toks) if t == 0 else []
                        if gi == 0:
                            wl.append(_bias_tok)
                        if gi >= 2:
                            wl.append(r_mm[gi - 2])
                        t_xt = LD.issue(
                            lambda eng, t=t, xb=xb: eng.dma_start(
                                out=s_xt[xb][:], in_=xh_image[t]),
                            wl)
                        wl2 = [r_epi[gi - 2]] if gi >= 2 else []
                        if gi <= 1:
                            wl2.append(_bias_tok)
                        t_res = LD.issue(
                            lambda eng, t=t, xb=xb, half=half: eng.dma_start(
                                out=s_res[xb][:],
                                in_=res_dram[t * 128:(t + 1) * 128,
                                             half * 2048:(half + 1) * 2048]),
                            wl2)
                        wseed = [t_xt, (gp, 3), _bias_tok]
                        if gi >= 2:
                            wseed.append(r_epi[gi - 2])

                        def seeds(eng, bi=bi, half=half):
                            last = None
                            for nb4 in range(4):
                                cols = slice(half * 2048 + nb4 * 512,
                                             half * 2048 + nb4 * 512 + 512)
                                pcol = slice(nb4 * 512, nb4 * 512 + 512)
                                eng.matmul(p_acc[bi][:, pcol], c_ones1[:], c_bh[:, cols],
                                           start=True, stop=False)
                                last = eng.matmul(p_acc[bi][:, pcol], c_ones1[:],
                                                  c_bl[:, cols], start=False, stop=False)
                            return last
                        plan.op("tensor", seeds, wseed, ())

                        def mms(eng, bi=bi, xb=xb):
                            last = None
                            for kc in range(32):
                                for nb4 in range(4):
                                    pcol = slice(nb4 * 512, nb4 * 512 + 512)
                                    wcol = slice(kc * 2048 + nb4 * 512,
                                                 kc * 2048 + nb4 * 512 + 512)
                                    last = eng.matmul(
                                        p_acc[bi][:, pcol],
                                        s_xt[xb][:, kc * 128:(kc + 1) * 128],
                                        s_wth[:, wcol],
                                        start=False, stop=(kc == 31))
                            return last
                        plan.op("tensor", mms, [], [(pe, 1)])
                        r_mm[gi] = pe.rec()
                        wep = [r_mm[gi], t_res]
                        if gi >= 2:
                            wep.append(r_sto[gi - 2])
                        plan.op("vector",
                                lambda eng, bi=bi, xb=xb: eng.scalar_tensor_tensor(
                                    s_o[bi][:], p_acc[bi][:], 0.5, s_res[xb][:],
                                    ALU.mult, ALU.add),
                                wep, [(dv, 1)])
                        r_epi[gi] = dv.rec()
                        r_sto[gi] = ST.issue(
                            lambda eng, t=t, bi=bi, half=half: eng.dma_start(
                                out=out_dram[t * 128:(t + 1) * 128,
                                             half * 2048:(half + 1) * 2048],
                                in_=s_o[bi][:]),
                            [r_epi[gi]])
                fence()

        plan_ln(x_in, NT, d_x1h, tag="a")
        plan_ffn(d_x1h, x_in, d_x2, NT, tag="a")

        plan_ln(d_x2, NT, None, xh_plain=d_x2h, tag="b")

        with ExitStack() as ph:
            PE_ = ph.enter_context
            s_pan = PE_(nc.sbuf_tensor("p2_pan", [128, 32 * 2048], BF16))
            s_wb = [PE_(nc.sbuf_tensor(f"p2_wb{i}", [128, 32 * 128], BF16))
                    for i in range(2)]
            s_prj = [PE_(nc.sbuf_tensor(f"p2_prj{i}", [128, 2048], BF16))
                     for i in range(2)]
            s_t1 = PE_(nc.sbuf_tensor("p2_t1", [64, 1024], F32))
            s_t2 = PE_(nc.sbuf_tensor("p2_t2", [64, 1024], F32))
            s_csA = PE_(nc.sbuf_tensor("p2_csA", [128, 2048], F32))
            s_csB = PE_(nc.sbuf_tensor("p2_csB", [128, 2048], F32))
            s_pcp = PE_(nc.sbuf_tensor("p2_pcp", [128, 2048], F32))
            _p2p = PE_(nc.psum_tensor("p2_p0", [128, 2048], F32))
            p_acc = [_p2p, _p2p]

            r_mm2 = {}
            r_pcons = {}
            r_scons = {}
            r_ev2 = {}
            prev_rope_tok = None
            for th in range(2):
                chunks = [("k", i) for i in range(HK)] + [("v", i) for i in range(HK)]
                if th == 0:
                    chunks += [("q", i) for i in range(HQ)]
                wcs = []
                if th == 1:
                    wcs = [prev_rope_tok]
                cs_toks = [
                    LD.issue(lambda eng, th=th: eng.dma_start(
                        out=s_csA[0:64, :], in_=cosr[:, th * 2048:(th + 1) * 2048]), wcs),
                    LD.issue(lambda eng, th=th: eng.dma_start(
                        out=s_csA[64:128, :], in_=sinr[:, th * 2048:(th + 1) * 2048])),
                    LD.issue(lambda eng, th=th: eng.dma_start(
                        out=s_csB[0:64, :], in_=sinr[:, th * 2048:(th + 1) * 2048])),
                    LD.issue(lambda eng, th=th: eng.dma_start(
                        out=s_csB[64:128, :], in_=cosr[:, th * 2048:(th + 1) * 2048])),
                ]
                pan_toks = []
                for kc in range(32):
                    w = []
                    if th == 1 and kc == 0:
                        w = [r_mm2[(0, 47)]]
                    pan_toks.append(LD.issue(
                        lambda eng, kc=kc, th=th: eng.dma_start(
                            out=s_pan[:, kc * 2048:(kc + 1) * 2048],
                            in_=d_x2h[th * 2048:(th + 1) * 2048,
                                      kc * 128:(kc + 1) * 128],
                            transpose=True),
                        w))
                for ci, (kind, hh) in enumerate(chunks):
                    bi = ci % 2
                    wsrc = {"k": wk, "v": wv, "q": wq}[kind]
                    w = [r_mm2[(th, ci - 2)]] if ci >= 2 else (
                        [r_mm2[(0, 47)]] if th == 1 else [])
                    wb_toks = []
                    for part in range(4):
                        src = wsrc.rearrange("(kc p) n -> p kc n", p=128)[
                            :, part * 8:(part + 1) * 8, hh * 128:(hh + 1) * 128]
                        dstv = s_wb[bi][:].rearrange("p (kc n) -> p kc n", n=128)[
                            :, part * 8:(part + 1) * 8, :]
                        wb_toks.append(GD.issue(
                            lambda eng, src=src, dstv=dstv: eng.dma_start(
                                out=dstv, in_=src),
                            w if part == 0 else []))
                    wmm = wb_toks + pan_toks
                    if ci >= 1:
                        wmm.append(r_pcons[(th, ci - 1)])
                    elif th == 1:
                        wmm.append(r_pcons[(0, 47)])

                    def mms(eng, bi=bi):
                        last = None
                        for kc in range(32):
                            for nb4 in range(4):
                                pcol = slice(nb4 * 512, nb4 * 512 + 512)
                                last = eng.matmul(
                                    p_acc[bi][:, pcol],
                                    s_wb[bi][:, kc * 128:(kc + 1) * 128],
                                    s_pan[:, kc * 2048 + nb4 * 512:
                                          kc * 2048 + nb4 * 512 + 512],
                                    start=(kc == 0), stop=(kc == 31))
                        return last
                    plan.op("tensor", mms, wmm, [(pe, 1)])
                    r_mm2[(th, ci)] = pe.rec()

                    if kind in ("k", "q"):
                        plan.op("vector",
                                lambda eng, bi=bi: eng.tensor_copy(
                                    s_pcp[:], p_acc[bi][:]),
                                [r_mm2[(th, ci)]], [(dv, 1)])
                        r_pcons[(th, ci)] = dv.rec()

                        def rope(eng, bi=bi):
                            pa = s_pcp
                            steps = []
                            for q2 in range(2):
                                cs = slice(q2 * 1024, (q2 + 1) * 1024)
                                steps += [
                                    lambda e, cs=cs: e.tensor_mul(
                                        s_t1[:], pa[0:64, cs], s_csA[0:64, cs]),
                                    lambda e, cs=cs: e.tensor_mul(
                                        s_t2[:], pa[64:128, cs], s_csA[64:128, cs]),
                                    lambda e, cs=cs, bi=bi: e.tensor_sub(
                                        s_prj[bi][0:64, cs], s_t1[:], s_t2[:]),
                                    lambda e, cs=cs: e.tensor_mul(
                                        s_t1[:], pa[0:64, cs], s_csB[0:64, cs]),
                                    lambda e, cs=cs: e.tensor_mul(
                                        s_t2[:], pa[64:128, cs], s_csB[64:128, cs]),
                                    lambda e, cs=cs, bi=bi: e.tensor_add(
                                        s_prj[bi][64:128, cs], s_t1[:], s_t2[:]),
                                ]
                            return chained(eng, CDV, steps)
                        wr = list(cs_toks)
                        if ci >= 2:
                            wr.append(r_scons[(th, ci - 2)])
                        elif th == 1:
                            wr.append(r_scons[(0, 46 + ci)])
                        plan.op("vector", rope, wr, [(dv, 1)])
                        tokr = dv.rec()
                        prev_rope_tok = tokr
                        dst = d_k[hh, :, th * 2048:(th + 1) * 2048] if kind == "k" \
                            else d_q[hh]
                        r_scons[(th, ci)] = ST.issue(
                            lambda eng, dst=dst, bi=bi: eng.dma_start(
                                out=dst, in_=s_prj[bi][:]),
                            [tokr])
                    else:
                        wr = [r_mm2[(th, ci)]]
                        if ci >= 2:
                            wr.append(r_scons[(th, ci - 2)])
                        elif th == 1:
                            wr.append(r_scons[(0, 46 + ci)])
                        plan.op("vector",
                                lambda eng, bi=bi: eng.tensor_copy(
                                    s_prj[bi][:], p_acc[bi][:]),
                                wr, [(dv, 1)])
                        wev = dv.rec()
                        r_pcons[(th, ci)] = wev
                        dst = d_v[hh][:, th * 2048:(th + 1) * 2048]
                        r_scons[(th, ci)] = ST.issue(
                            lambda eng, dst=dst, bi=bi: eng.dma_start(
                                out=dst, in_=s_prj[bi][:]),
                            [wev])
            fence()

        with ExitStack() as ph:
            PE_ = ph.enter_context
            s_k = [PE_(nc.sbuf_tensor(f"a_k{i}", [128, TLOC], BF16)) for i in range(2)]
            s_v = [PE_(nc.sbuf_tensor(f"a_v{i}", [128, NT * 128], BF16))
                   for i in range(2)]
            s_q = [PE_(nc.sbuf_tensor(f"a_q{i}", [128, 4 * TOWN], BF16))
                   for i in range(2)]
            s_pt = [PE_(nc.sbuf_tensor(f"a_pt{i}", [128, 512], BF16)) for i in range(8)]
            s_m1 = [PE_(nc.sbuf_tensor(f"a_m1{i}", [128, 64], F32)) for i in range(2)]
            s_mcol = [PE_(nc.sbuf_tensor(f"a_mcol{i}", [128, 1], F32)) for i in range(2)]
            s_bias = [PE_(nc.sbuf_tensor(f"a_bias{i}", [128, 1], F32)) for i in range(2)]
            s_mred = [PE_(nc.sbuf_tensor(f"a_mred{i}", [1, 1], F32)) for i in range(2)]
            s_sumr = PE_(nc.sbuf_tensor("a_sumr", [1, 512], F32))
            s_rb = PE_(nc.sbuf_tensor("a_rb", [128, 512], F32))
            s_ot = PE_(nc.sbuf_tensor("a_ot", [128, 512], BF16))
            RING = 40
            s_ring = PE_(nc.sbuf_tensor("a_ring", [128, RING * 512], F32))
            p_s2 = [PE_(nc.psum_tensor(f"a_s2{i}", [128, 512], F32)) for i in range(5)]
            p_o = PE_(nc.psum_tensor("a_o", [128, 512], F32))
            p_sum = PE_(nc.psum_tensor("a_sum", [128, 512], F32))
            p_scr = PE_(nc.psum_tensor("a_scr", [128, 512], F32))

            def ktiles(qb):
                out = []
                for sp in range(4 * qb + 4):
                    for side in range(2):
                        col = sp + 16 * side
                        delta = sp - 4 * qb
                        r0 = max(0, delta) * 128
                        msk = None
                        if delta >= 0:
                            msk = c_mtri if side == 0 else c_mpar
                        out.append((col, r0, msk))
                return out

            LOOKAHEAD = 6
            items = []
            for g in range(HK):
                for hh in range(4):
                    for qb in range(4):
                        items.append((g, hh, qb))
            NITEM = len(items)

            g_toks = {}
            r_gend = {}
            r_copy = {}
            r_mskd = {}
            r_exp = {}
            r_avt = {}
            r_ptfree = {}
            it_state = {}
            kg1 = [0]
            kg2 = [0]
            prev_store = [None]
            prev_evo = [None]

            def emit_loads(g):
                w = [r_gend[g - 2]] if g >= 2 else []
                toks = [LD.issue(lambda eng, g=g: eng.dma_start(
                    out=s_k[g % 2][:], in_=d_k[g]), w)]
                for vt in range(NT):
                    toks.append(LD.issue(
                        lambda eng, g=g, vt=vt: eng.dma_start(
                            out=s_v[g % 2][:, vt * 128:(vt + 1) * 128],
                            in_=d_v[g][:, vt * 128:(vt + 1) * 128],
                            transpose=True)))
                for hh in range(4):
                    toks.append(LD.issue(
                        lambda eng, g=g, hh=hh: eng.dma_start(
                            out=s_q[g % 2][:, hh * TOWN:(hh + 1) * TOWN],
                            in_=d_q[4 * g + hh])))
                g_toks[g] = toks

            def gen_p1(m):
                g, hh, qb = items[m]
                gb = g % 2
                mb = m % 2
                kts = ktiles(qb)
                it_state[m] = {"kts": kts}
                first_of_g = (hh == 0 and qb == 0)
                lastred = None
                for i, (col, r0, msk) in enumerate(kts):
                    kg = kg1[0]
                    sb = kg % 5
                    slot = kg % RING
                    w1 = []
                    if first_of_g and i == 0:
                        w1 += g_toks[g]
                    if kg >= 5:
                        w1.append(r_copy[kg - 5])
                    plan.op("tensor",
                            lambda eng, gb=gb, col=col, sb=sb, hh=hh, qb=qb:
                            eng.matmul(
                                p_s2[sb][:, 0:512],
                                s_k[gb][:, col * 128:(col + 1) * 128],
                                s_q[gb][:, hh * TOWN + qb * 512:
                                        hh * TOWN + qb * 512 + 512],
                                start=True, stop=True),
                            w1, [(pe, 1)])
                    tokm = pe.rec()
                    rc = 0 if kg < RING else r0
                    wc = [tokm]
                    if kg >= RING:
                        wc.append(r_exp[kg - RING])
                    plan.op("scalar",
                            lambda eng, sb=sb, slot=slot, rc=rc: eng.activation(
                                s_ring[:, slot * 512 + rc:slot * 512 + 512],
                                p_s2[sb][:, rc:512], AF.Identity),
                            wc, [(ac, 1)])
                    r_copy[kg] = ac.rec()
                    if msk is not None:
                        plan.op("vector",
                                lambda eng, slot=slot, r0=r0, msk=msk:
                                eng.tensor_add(
                                    s_ring[:, slot * 512 + r0:
                                           slot * 512 + r0 + 128],
                                    s_ring[:, slot * 512 + r0:
                                           slot * 512 + r0 + 128], msk[:]),
                                [r_copy[kg]], [(dv, 1)])
                        r_mskd[kg] = dv.rec()
                    if i % 2 == 1:
                        pslot = (kg - 1) % RING
                        plan.op("vector",
                                lambda eng, i=i, mb=mb, pslot=pslot:
                                eng.tensor_reduce(
                                    s_m1[mb][:, i // 2:i // 2 + 1],
                                    s_ring[:, pslot * 512:pslot * 512 + 1024],
                                    AX.X, ALU.max),
                                [r_copy[kg - 1], r_copy[kg]], [(dv, 1)])
                        lastred = dv.rec()
                    kg1[0] += 1
                    yield
                nk = len(kts)
                plan.op("vector",
                        lambda eng, nk=nk, mb=mb: eng.tensor_reduce(
                            s_mcol[mb][:], s_m1[mb][:, 0:nk // 2], AX.X, ALU.max),
                        [lastred], [(dv, 1)])
                it_state[m]["comb"] = dv.rec()

            def emit_chain(m):
                st_ = it_state[m]
                mb = m % 2
                wtr = [st_["comb"]]
                if prev_evo[0] is not None:
                    wtr.append(prev_evo[0])
                plan.op("tensor",
                        lambda eng, mb=mb: eng.transpose(
                            p_scr[0:1, 0:128], s_mcol[mb][:], c_idf[:]),
                        wtr, [(pe, 1)])
                tok2 = pe.rec()
                plan.op("vector",
                        lambda eng, mb=mb: eng.tensor_reduce(
                            s_mred[mb][:], p_scr[0:1, 0:128], AX.X, ALU.max),
                        [tok2], [(dv, 1)])
                tok3 = dv.rec()
                plan.op("tensor",
                        lambda eng, mb=mb: eng.matmul(
                            p_scr[:, 0:1], c_onesf[:], s_mred[mb][:],
                            start=True, stop=True),
                        [tok3], [(pe, 1)])
                tok4 = pe.rec()

                def mkbias(eng, mb=mb):
                    return chained(eng, CDV, [
                        lambda e: e.tensor_copy(s_mcol[mb][:], p_scr[:, 0:1]),
                        lambda e: e.tensor_scalar(s_bias[mb][:], s_mcol[mb][:],
                                                  -SCL, None, ALU.mult),
                    ])
                plan.op("vector", mkbias, [tok4], [(dv, 1)])
                st_["bias"] = dv.rec()

            def gen_p2(m):
                g, hh, qb = items[m]
                gb = g % 2
                mb = m % 2
                kts = it_state[m]["kts"]
                nk = len(kts)
                pend = []
                avq = []

                def emit_av(ent, i_now):
                    ptslot, r0, kgx, li, col, first, last = ent

                    def av(eng, gb=gb, col=col, r0=r0, ptslot=ptslot,
                           first=first, last=last):
                        return eng.matmul(p_o[:, r0:512],
                                          s_v[gb][:, col * 128:(col + 1) * 128],
                                          s_pt[ptslot][:, r0:512],
                                          start=first, stop=last)
                    wav_ = [r_exp[kgx]]
                    if first and prev_evo[0] is not None:
                        wav_.append(prev_evo[0])
                    plan.op("tensor", av, wav_, [(pe, 1)])
                    r_avt[kgx] = pe.rec()
                    pend.append((ptslot, r0, kgx, li))
                    if len(pend) == 3 or last:
                        def sums(eng, pend=tuple(pend), nk=nk):
                            lastins = None
                            for (sb_, r0_, _, lj) in pend:
                                j = lj % 3
                                lastins = eng.matmul(
                                    p_sum[32 * j:32 * j + 1, r0_:512],
                                    c_onescol[:], s_pt[sb_][:, r0_:512],
                                    start=(lj < 3), stop=(lj + 3 >= nk),
                                    tile_position=(0, 32 * j))
                            return lastins
                        plan.op("tensor", sums, [r_exp[pend[-1][2]]], [(pe, 1)])
                        grp_tok = pe.rec()
                        for (_, _, kgx2, _) in pend:
                            r_ptfree[kgx2] = grp_tok
                        del pend[:]

                for i, (col, r0, msk) in enumerate(kts):
                    kg = kg2[0]
                    slot = kg % RING
                    ptslot = kg % 8
                    wexp = []
                    if i == 0:
                        wexp.append(it_state[m]["bias"])
                    if kg in r_mskd:
                        wexp.append(r_mskd[kg])
                    if kg >= 8:
                        wexp.append(r_ptfree[kg - 8])
                    plan.op("scalar",
                            lambda eng, slot=slot, r0=r0, mb=mb, ptslot=ptslot:
                            eng.activation(
                                s_pt[ptslot][:, r0:512],
                                s_ring[:, slot * 512 + r0:slot * 512 + 512],
                                AF.Exp, bias=s_bias[mb][:], scale=SCL),
                            wexp, [(ac, 1)])
                    r_exp[kg] = ac.rec()
                    avq.append((ptslot, r0, kg, i, col, i == 0, i == nk - 1))
                    if len(avq) > 3:
                        emit_av(avq.pop(0), i)
                    kg2[0] += 1
                    yield
                while avq:
                    emit_av(avq.pop(0), nk - 1)
                wav = r_avt[kg2[0] - 1]
                minr0 = [512, 512, 512]
                for li, (_, r0_, _) in enumerate(kts):
                    j = li % 3
                    minr0[j] = min(minr0[j], r0_)

                def sums_fin(eng, minr0=tuple(minr0)):
                    steps = [lambda e: e.memset(s_sumr[:], 0.0)]
                    for j in range(3):
                        if minr0[j] < 512:
                            steps.append(
                                lambda e, j=j, a=minr0[j]: e.scalar_tensor_tensor(
                                    s_sumr[:, a:512], p_sum[32 * j:32 * j + 1, a:512],
                                    1.0, s_sumr[:, a:512], ALU.mult, ALU.add))
                    steps += [
                        lambda e: e.tensor_scalar(s_sumr[:], s_sumr[:], 1e-30,
                                                  None, ALU.add),
                        lambda e: e.reciprocal(s_sumr[:], s_sumr[:]),
                    ]
                    return chained(eng, CDV, steps)
                plan.op("vector", sums_fin, [wav], [(dv, 1)])
                tok5 = dv.rec()
                plan.op("tensor",
                        lambda eng: eng.matmul(p_scr[:], c_onesf[:], s_sumr[:],
                                               start=True, stop=True),
                        [tok5], [(pe, 1)])
                tok6 = pe.rec()

                def evo(eng):
                    return chained(eng, CDV, [
                        lambda e: e.tensor_copy(s_rb[:], p_scr[:]),
                        lambda e: e.tensor_mul(s_ot[:], p_o[:], s_rb[:]),
                    ])
                wevo = [tok6]
                if prev_store[0] is not None:
                    wevo.append(prev_store[0])
                plan.op("vector", evo, wevo, [(dv, 1)])
                prev_evo[0] = dv.rec()
                dst = d_ot[(4 * g + hh) * 128:(4 * g + hh + 1) * 128,
                           qb * 512:(qb + 1) * 512]
                prev_store[0] = ST.issue(
                    lambda eng, dst=dst: eng.dma_start(out=dst, in_=s_ot[:]),
                    [prev_evo[0]])
                if hh == 3 and qb == 3:
                    r_gend[g] = pe.rec()
                yield

            def drain(gen):
                for _ in gen:
                    pass

            emit_loads(0)
            emit_loads(1)
            drain(gen_p1(0))
            for m in range(NITEM):
                g, hh, qb = items[m]
                if hh == 0 and qb == 1 and (g + 1) < HK and (g + 1) not in g_toks:
                    emit_loads(g + 1)
                A = gen_p1(m + 1) if m + 1 < NITEM else iter(())
                for _ in range(LOOKAHEAD):
                    if next(A, "end") == "end":
                        break
                emit_chain(m)
                B = gen_p2(m)
                alive = True
                while alive:
                    alive = False
                    if next(A, "end") != "end":
                        alive = True
                    if next(B, "end") != "end":
                        alive = True
            fence()

        with ExitStack() as ph:
            PE_ = ph.enter_context
            s_wo = PE_(nc.sbuf_tensor("w_wo", [128, 32 * 2048], BF16))
            s_otl = [PE_(nc.sbuf_tensor(f"w_ot{i}", [128, D], BF16)) for i in range(2)]
            s_res = [PE_(nc.sbuf_tensor(f"w_r{i}", [128, 2048], F32)) for i in range(2)]
            s_o = [PE_(nc.sbuf_tensor(f"w_o{i}", [128, 2048], F32)) for i in range(2)]
            p_acc = [PE_(nc.psum_tensor(f"w_p{i}", [128, 2048], F32)) for i in range(2)]

            r_mm5 = {}
            r_epi5 = {}
            r_sto5 = {}
            for half in range(2):
                ww = [r_mm5[NTOWN - 1]] if half == 1 else []
                wo_toks = []
                for cc in range(32):
                    src = wo.rearrange("(cc p) n -> cc p n", p=128)[
                        cc, :, half * 2048:(half + 1) * 2048]
                    wo_toks.append(GD.issue(
                        lambda eng, src=src, cc=cc: eng.dma_start(
                            out=s_wo[:, cc * 2048:(cc + 1) * 2048], in_=src),
                        ww if cc == 0 else []))
                for t in range(NTOWN):
                    gi = half * NTOWN + t
                    bi = gi % 2
                    xb = gi % 2
                    wl = list(wo_toks) if t == 0 else []
                    if gi >= 2:
                        wl.append(r_mm5[gi - 2])
                    ot_toks = []
                    for part in range(4):
                        src = d_ot.rearrange("(cc p) (t n) -> cc p t n", p=128, n=128)[
                            part * 8:(part + 1) * 8, :, t, :].rearrange(
                            "cc p n -> p cc n")
                        dstv = s_otl[xb][:].rearrange("p (cc n) -> p cc n", n=128)[
                            :, part * 8:(part + 1) * 8, :]
                        ot_toks.append(LD.issue(
                            lambda eng, src=src, dstv=dstv: eng.dma_start(
                                out=dstv, in_=src),
                            wl if part == 0 else []))
                    wl2 = [r_epi5[gi - 2]] if gi >= 2 else []
                    t_res = LD.issue(
                        lambda eng, t=t, xb=xb, half=half: eng.dma_start(
                            out=s_res[xb][:],
                            in_=d_x2[t * 128:(t + 1) * 128,
                                     half * 2048:(half + 1) * 2048]),
                        wl2)
                    wmm = list(ot_toks)
                    if gi >= 2:
                        wmm.append(r_epi5[gi - 2])

                    def mms(eng, bi=bi, xb=xb):
                        last = None
                        for cc in range(32):
                            for nb4 in range(4):
                                pcol = slice(nb4 * 512, nb4 * 512 + 512)
                                wcol = slice(cc * 2048 + nb4 * 512,
                                             cc * 2048 + nb4 * 512 + 512)
                                last = eng.matmul(
                                    p_acc[bi][:, pcol],
                                    s_otl[xb][:, cc * 128:(cc + 1) * 128],
                                    s_wo[:, wcol],
                                    start=(cc == 0), stop=(cc == 31))
                        return last
                    plan.op("tensor", mms, wmm, [(pe, 1)])
                    r_mm5[gi] = pe.rec()
                    wep = [r_mm5[gi], t_res]
                    if gi >= 2:
                        wep.append(r_sto5[gi - 2])
                    plan.op("vector",
                            lambda eng, bi=bi, xb=xb: eng.tensor_add(
                                s_o[bi][:], p_acc[bi][:], s_res[xb][:]),
                            wep, [(dv, 1)])
                    r_epi5[gi] = dv.rec()
                    r_sto5[gi] = ST.issue(
                        lambda eng, t=t, bi=bi, half=half: eng.dma_start(
                            out=d_x3[t * 128:(t + 1) * 128,
                                     half * 2048:(half + 1) * 2048],
                            in_=s_o[bi][:]),
                        [r_epi5[gi]])
            fence()

        plan_ln(d_x3, NTOWN, d_x3h, tag="c")
        plan_ffn(d_x3h, d_x3, o_out, NTOWN, tag="c")

        plan.wait_all("sync", SEMS)

        with nc.Block() as blk:
            @blk.sync
            def _(eng):
                for f in plan.q["sync"]:
                    f(eng)

            @blk.gpsimd
            def _(eng):
                for f in plan.q["gpsimd"]:
                    f(eng)

            @blk.tensor
            def _(eng):
                for f in plan.q["tensor"]:
                    f(eng)

            @blk.vector
            def _(eng):
                for f in plan.q["vector"]:
                    f(eng)

            @blk.scalar
            def _(eng):
                for f in plan.q["scalar"]:
                    f(eng)

    return nc


def _host_inputs(inputs):
    X = np.asarray(inputs["L0_attn_X"])
    Wq = np.ascontiguousarray(np.asarray(inputs["L0_attn_Wq"]))
    Wk = np.ascontiguousarray(np.asarray(inputs["L0_attn_Wk"]))
    Wv = np.ascontiguousarray(np.asarray(inputs["L0_attn_Wv"]))
    Wo = np.ascontiguousarray(np.asarray(inputs["L0_attn_Wo"]))
    Wqi = np.ascontiguousarray(np.asarray(inputs["L0_ffn_weight_q"]))
    Zp = np.ascontiguousarray(np.asarray(inputs["L0_ffn_zp"]))
    Sc = np.ascontiguousarray(np.asarray(inputs["L0_ffn_scale"]))
    Bias = np.ascontiguousarray(np.asarray(inputs["L0_ffn_bias"]).reshape(1, D))
    ident = np.eye(128, dtype=np.float32)

    inv_freq = (1.0 / (10000.0 ** (np.arange(0, 64, dtype=np.float32) / 64.0))) \
        .astype(np.float32)
    tri = np.where(np.arange(128)[:, None] <= np.arange(128)[None, :], 0.0, NEG) \
        .astype(np.float32)

    in_maps = []
    perms = []
    for c in range(8):
        b, p = c // 2, c % 2
        own = np.concatenate([np.arange(128 * (2 * s + p), 128 * (2 * s + p) + 128)
                              for s in range(16)])
        oth = np.concatenate([np.arange(128 * (2 * s + 1 - p),
                                        128 * (2 * s + 1 - p) + 128)
                              for s in range(16)])
        perm = np.concatenate([own, oth])
        perms.append(own)
        pos = perm.astype(np.float32)
        ang = pos[None, :] * inv_freq[:, None]
        cosr_ = np.cos(ang).astype(np.float32)
        sinr_ = np.sin(ang).astype(np.float32)
        mpar = np.zeros((128, 128), np.float32) if p == 1 \
            else np.full((128, 128), NEG, np.float32)
        in_maps.append(dict(
            x_in=np.ascontiguousarray(X[b][perm]),
            wq=Wq, wk=Wk, wv=Wv, wo=Wo, wqint=Wqi, zp=Zp, qsc=Sc, fbias=Bias,
            cosr=cosr_, sinr=sinr_, mtri=tri, mpar=mpar, identf=ident,
        ))
    return in_maps, perms


def kernel(**inputs):
    in_maps, perms = _host_inputs(inputs)
    nc = build_nc()
    res = run_bass_kernel_spmd(nc, in_maps, core_ids=list(range(8)))
    out = np.empty((4, 4096, 4096), dtype=np.float32)
    for c in range(8):
        b = c // 2
        out[b][perms[c]] = res.results[c]["o_out"]
    return out


if __name__ == "__main__":
    import reference as _ref
    ins = _ref.setup_inputs(0)
    got = kernel(**{k: np.asarray(v) for k, v in ins.items()})
    exp = np.asarray(_ref.reference(**ins))
    num = float(np.mean((got - exp) ** 2))
    den = float(np.mean(exp ** 2))
    print("resvar:", num / den)


# version 13
# speedup vs baseline: 1.0282x; 1.0282x over previous
import numpy as np
import ml_dtypes
from contextlib import ExitStack

import concourse.bass as bass
import concourse.mybir as mybir
from concourse.bass_utils import run_bass_kernel_spmd

F32 = mybir.dt.float32
BF16 = mybir.dt.bfloat16
I32 = mybir.dt.int32
AF = mybir.ActivationFunctionType
ALU = mybir.AluOpType
AX = mybir.AxisListType

D = 4096
NT = 32
NTOWN = 16
TLOC = NT * 128
TOWN = NTOWN * 128
HK = 8
HQ = 32
NGRP = 64
EPS = 1e-5
SCL = float(1.0 / np.sqrt(128.0))
NEG = -1.0e30

ENGINES = ("sync", "gpsimd", "tensor", "vector", "scalar")


class S:
    __slots__ = ("h", "n", "name")

    def __init__(self, h, name):
        self.h = h
        self.n = 0
        self.name = name

    def rec(self):
        return (self, self.n)


def _mkop(waits, fn, incs):
    def run(eng):
        for (h, v) in waits:
            eng.wait_ge(h, v)
        ins = fn(eng)
        for (h, k) in incs:
            ins.then_inc(h, k)
    return run


class Plan:
    def __init__(self):
        self.q = {e: [] for e in ENGINES}
        self.selfsem = {}

    def op(self, e, fn, waits=(), incs=()):
        waits = list(waits)
        ss = self.selfsem.get(e)
        if ss is not None and ss.n > 0:
            waits.append((ss, ss.n))
        if ss is not None and not incs:
            incs = [(ss, 1)]
        w = tuple((s.h, int(v)) for (s, v) in waits if v > 0)
        ii = []
        for (s, k) in incs:
            s.n += k
            ii.append((s.h, k))
        self.q[e].append(_mkop(w, fn, tuple(ii)))

    def wait_all(self, e, sems):
        w = tuple((s.h, s.n) for s in sems if s.n > 0)

        def run(eng):
            for (h, v) in w:
                eng.wait_ge(h, v)
        self.q[e].append(run)


class CC:

    def __init__(self, h):
        self.h = h
        self.n = 0

    def link(self, ins):
        ins.then_inc(self.h, 1)
        self.n += 1

    def step(self, eng, ins, last=False):
        self.link(ins)

    def barrier(self, eng):
        eng.wait_ge(self.h, self.n)


def chained(eng, cc, steps):
    last = None
    for k, f in enumerate(steps):
        if k > 0:
            cc.link(last)
            cc.barrier(eng)
        last = f(eng)
    return last


class Chan:

    def __init__(self, plan, eng, sems):
        self.plan = plan
        self.eng = eng
        self.sems = sems
        self.i = 0

    def issue(self, fn, waits=()):
        s = self.sems[self.i % len(self.sems)]
        w = [(s, s.n)] + list(waits)
        self.plan.op(self.eng, fn, w, [(s, 16)])
        self.i += 1
        return (s, s.n)


def build_nc():
    nc = bass.Bass("TRN2", target_bir_lowering=False, debug=False)

    x_in = nc.dram_tensor("x_in", [TLOC, D], F32, kind="ExternalInput").ap()
    wq = nc.dram_tensor("wq", [D, D], F32, kind="ExternalInput").ap()
    wk = nc.dram_tensor("wk", [D, HK * 128], F32, kind="ExternalInput").ap()
    wv = nc.dram_tensor("wv", [D, HK * 128], F32, kind="ExternalInput").ap()
    wo = nc.dram_tensor("wo", [D, D], F32, kind="ExternalInput").ap()
    wqint = nc.dram_tensor("wqint", [D, D], I32, kind="ExternalInput").ap()
    zp = nc.dram_tensor("zp", [D, NGRP], F32, kind="ExternalInput").ap()
    qsc = nc.dram_tensor("qsc", [D, NGRP], F32, kind="ExternalInput").ap()
    fbias = nc.dram_tensor("fbias", [1, D], F32, kind="ExternalInput").ap()
    cosr = nc.dram_tensor("cosr", [64, TLOC], F32, kind="ExternalInput").ap()
    sinr = nc.dram_tensor("sinr", [64, TLOC], F32, kind="ExternalInput").ap()
    mtri = nc.dram_tensor("mtri", [128, 128], F32, kind="ExternalInput").ap()
    mpar = nc.dram_tensor("mpar", [128, 128], F32, kind="ExternalInput").ap()
    identf = nc.dram_tensor("identf", [128, 128], F32, kind="ExternalInput").ap()

    o_out = nc.dram_tensor("o_out", [TOWN, D], F32, kind="ExternalOutput").ap()

    d_wt = nc.dram_tensor("d_wt", [D, D], BF16, kind="Internal").ap()
    d_x1h = nc.dram_tensor("d_x1h", [NT, 128, D], BF16, kind="Internal").ap()
    d_x2 = nc.dram_tensor("d_x2", [TLOC, D], F32, kind="Internal").ap()
    d_x2h = nc.dram_tensor("d_x2h", [TLOC, D], BF16, kind="Internal").ap()
    d_k = nc.dram_tensor("d_k", [HK, 128, TLOC], BF16, kind="Internal").ap()
    d_v = nc.dram_tensor("d_v", [HK, 128, TLOC], BF16, kind="Internal").ap()
    d_q = nc.dram_tensor("d_q", [HQ, 128, TOWN], BF16, kind="Internal").ap()
    d_ot = nc.dram_tensor("d_ot", [D, TOWN], BF16, kind="Internal").ap()
    d_x3 = nc.dram_tensor("d_x3", [TOWN, D], F32, kind="Internal").ap()
    d_x3h = nc.dram_tensor("d_x3h", [NTOWN, 128, D], BF16, kind="Internal").ap()

    plan = Plan()

    with ExitStack() as top:
        E = top.enter_context
        pe = S(E(nc.semaphore("s_pe")), "pe")
        dv = S(E(nc.semaphore("s_dv")), "dv")
        ac = S(E(nc.semaphore("s_ac")), "ac")
        gp = S(E(nc.semaphore("s_gp")), "gp")
        lds = [S(E(nc.semaphore(f"s_ld{i}")), f"ld{i}") for i in range(6)]
        gds = [S(E(nc.semaphore(f"s_gd{i}")), f"gd{i}") for i in range(4)]
        sts = [S(E(nc.semaphore(f"s_st{i}")), f"st{i}") for i in range(4)]
        SEMS = tuple([pe, dv, ac, gp] + lds + gds + sts)

        plan.selfsem = {"vector": dv, "scalar": ac, "tensor": pe, "gpsimd": gp}
        CDV = CC(E(nc.semaphore("s_cdv")))
        CAC = CC(E(nc.semaphore("s_cac")))
        LD = Chan(plan, "sync", lds)
        GD = Chan(plan, "gpsimd", gds)
        ST = Chan(plan, "sync", sts)

        def fence():
            for e in ENGINES:
                plan.wait_all(e, SEMS)

        c_ones1 = E(nc.sbuf_tensor("c_ones1", [1, 128], BF16))
        c_onesf = E(nc.sbuf_tensor("c_onesf", [1, 128], F32))
        c_onescol = E(nc.sbuf_tensor("c_onescol", [128, 1], BF16))
        c_idb = E(nc.sbuf_tensor("c_idb", [128, 128], BF16))
        c_idf = E(nc.sbuf_tensor("c_idf", [128, 128], F32))
        c_mtri = E(nc.sbuf_tensor("c_mtri", [128, 128], F32))
        c_mpar = E(nc.sbuf_tensor("c_mpar", [128, 128], F32))

        LD.issue(lambda eng: eng.dma_start(out=c_mtri[:], in_=mtri))
        LD.issue(lambda eng: eng.dma_start(out=c_mpar[:], in_=mpar))
        LD.issue(lambda eng: eng.dma_start(out=c_idf[:], in_=identf))
        GD.issue(lambda eng: eng.dma_start(out=c_idb[:], in_=identf))
        plan.op("gpsimd", lambda eng: eng.memset(c_ones1[:], 1.0), (), [(gp, 1)])
        plan.op("gpsimd", lambda eng: eng.memset(c_onesf[:], 1.0), (), [(gp, 1)])
        plan.op("gpsimd", lambda eng: eng.memset(c_onescol[:], 1.0), (), [(gp, 1)])
        fence()

        with ExitStack() as ph:
            PE_ = ph.enter_context
            s_wq = [PE_(nc.sbuf_tensor(f"p0_wq{i}", [128, D], BF16)) for i in range(2)]
            s_zp = [PE_(nc.sbuf_tensor(f"p0_zp{i}", [128, NGRP], F32)) for i in range(2)]
            s_sc = [PE_(nc.sbuf_tensor(f"p0_sc{i}", [128, NGRP], F32)) for i in range(2)]
            s_wd = [PE_(nc.sbuf_tensor(f"p0_wd{i}", [128, D], BF16)) for i in range(2)]
            s_wt = [PE_(nc.sbuf_tensor(f"p0_wt{i}", [128, D], BF16)) for i in range(2)]
            p_tr = [PE_(nc.psum_tensor(f"p0_ptr{i}", [128, 1024], BF16)) for i in range(4)]

            r_deq = {}
            r_tr = {}
            r_ev = {}
            r_st = {}
            for nb in range(NT):
                bi = nb % 2
                w_gd = [r_deq[nb - 2]] if nb >= 2 else []
                t_wq = GD.issue(
                    lambda eng, nb=nb, bi=bi: eng.dma_start(
                        out=s_wq[bi][:], in_=wqint[nb * 128:(nb + 1) * 128, :]),
                    w_gd)
                t_zp = LD.issue(
                    lambda eng, nb=nb, bi=bi: eng.dma_start(
                        out=s_zp[bi][:], in_=zp[nb * 128:(nb + 1) * 128, :]),
                    w_gd)
                t_sc = LD.issue(
                    lambda eng, nb=nb, bi=bi: eng.dma_start(
                        out=s_sc[bi][:], in_=qsc[nb * 128:(nb + 1) * 128, :]))
                w0 = [t_wq, t_zp, t_sc]
                if nb >= 2:
                    w0.append(r_tr[(nb - 2, 3)])
                for g in range(NGRP):
                    inc = [(dv, 1)] if g == NGRP - 1 else []
                    plan.op("vector",
                            lambda eng, bi=bi, g=g: eng.tensor_scalar(
                                s_wd[bi][:, g * 64:(g + 1) * 64],
                                s_wq[bi][:, g * 64:(g + 1) * 64],
                                s_zp[bi][:, g:g + 1], s_sc[bi][:, g:g + 1],
                                ALU.subtract, ALU.mult),
                            w0 if g == 0 else (), inc)
                r_deq[nb] = dv.rec()
                for grpi in range(4):
                    w = [r_deq[nb]] if grpi == 0 else []
                    if nb >= 1:
                        w.append(r_ev[nb - 1])
                    if nb >= 2 and grpi == 0:
                        w.extend(r_st[nb - 2])
                    for j in range(8):
                        kc = grpi * 8 + j
                        inc = [(pe, 1)] if j == 7 else []
                        plan.op("tensor",
                                lambda eng, bi=bi, kc=kc, grpi=grpi, j=j: eng.transpose(
                                    p_tr[grpi][:, j * 128:(j + 1) * 128],
                                    s_wd[bi][:, kc * 128:(kc + 1) * 128], c_idb[:]),
                                w if j == 0 else (), inc)
                    r_tr[(nb, grpi)] = pe.rec()
                    plan.op("vector",
                            lambda eng, bi=bi, grpi=grpi: eng.tensor_copy(
                                s_wt[bi][:, grpi * 1024:(grpi + 1) * 1024],
                                p_tr[grpi][:, :]),
                            [r_tr[(nb, grpi)]], [(dv, 1)])
                r_ev[nb] = dv.rec()
                stoks = []
                for part in range(4):
                    dst = d_wt.rearrange("(kc p) (nb n) -> p kc nb n", p=128, n=128)[
                        :, part * 8:(part + 1) * 8, nb, :]
                    srcv = s_wt[bi][:].rearrange("p (kc n) -> p kc n", n=128)[
                        :, part * 8:(part + 1) * 8, :]
                    stoks.append(ST.issue(
                        lambda eng, dst=dst, srcv=srcv: eng.dma_start(out=dst, in_=srcv),
                        [r_ev[nb]] if part == 0 else []))
                r_st[nb] = stoks
            fence()

        def plan_ln(src_dram, n_tiles, dst_image, xh_plain=None, tag=""):
            with ExitStack() as ph:
                PE_ = ph.enter_context
                s_x = [PE_(nc.sbuf_tensor(f"ln_x{tag}{i}", [128, D], F32))
                       for i in range(3)]
                s_scr = [PE_(nc.sbuf_tensor(f"ln_scr{tag}{i}", [128, D], F32))
                         for i in range(2)]
                s_xh = [PE_(nc.sbuf_tensor(f"ln_xh{tag}{i}", [128, D], BF16))
                        for i in range(2)]
                s_st2 = [PE_(nc.sbuf_tensor(f"ln_st{tag}{i}", [128, 8], F32))
                         for i in range(2)]
                if dst_image is not None:
                    p_tr = [PE_(nc.psum_tensor(f"ln_ptr{tag}{i}", [128, 1024], BF16))
                            for i in range(4)]
                    s_wtile = [PE_(nc.sbuf_tensor(f"ln_wt{tag}{i}", [128, D], BF16))
                               for i in range(2)]
                r_norm = {}
                r_trl = {}
                r_evl = {}
                r_stl = {}
                r_fin = {}
                for t in range(n_tiles):
                    bi = t % 2
                    xb = t % 3
                    w = [r_norm[t - 3]] if t >= 3 else []
                    t_x = LD.issue(
                        lambda eng, t=t, xb=xb: eng.dma_start(
                            out=s_x[xb][:], in_=src_dram[t * 128:(t + 1) * 128, :]),
                        w)
                    wstat = [t_x]
                    if t >= 2:
                        wstat.append(r_fin[t - 2])
                    plan.op("vector",
                            lambda eng, xb=xb, bi=bi: eng.tensor_reduce(
                                s_st2[bi][:, 0:1], s_x[xb][:], AX.X, ALU.add),
                            wstat, [(dv, 1)])
                    wsq = [t_x]
                    if t >= 2:
                        wsq.append(r_norm[t - 2])
                    plan.op("scalar",
                            lambda eng, xb=xb, bi=bi: eng.activation(
                                s_scr[bi][:], s_x[xb][:], AF.Square,
                                accum_out=s_st2[bi][:, 1:2]),
                            wsq, [(ac, 1)])
                    wac = ac.rec()

                    def fin(eng, bi=bi):
                        s = s_st2[bi]
                        return chained(eng, CDV, [
                            lambda e: e.tensor_scalar(s[:, 2:3], s[:, 0:1], 1.0 / D,
                                                      None, ALU.mult),
                            lambda e: e.tensor_scalar(s[:, 3:4], s[:, 1:2], 1.0 / D,
                                                      None, ALU.mult),
                            lambda e: e.tensor_mul(s[:, 7:8], s[:, 2:3], s[:, 2:3]),
                            lambda e: e.tensor_sub(s[:, 3:4], s[:, 3:4], s[:, 7:8]),
                            lambda e: e.tensor_scalar(s[:, 3:4], s[:, 3:4], EPS,
                                                      None, ALU.add),
                        ])
                    plan.op("vector", fin, [wac], [(dv, 1)])
                    wdvf = dv.rec()

                    def rstd(eng, bi=bi):
                        s = s_st2[bi]
                        return chained(eng, CAC, [
                            lambda e: e.activation(s[:, 4:5], s[:, 3:4], AF.Ln),
                            lambda e: e.activation(s[:, 5:6], s[:, 4:5], AF.Exp,
                                                   scale=-0.5),
                        ])
                    plan.op("scalar", rstd, [wdvf], [(ac, 1)])
                    wacr = ac.rec()

                    def nbias(eng, bi=bi):
                        s = s_st2[bi]
                        return chained(eng, CDV, [
                            lambda e: e.tensor_mul(s[:, 6:7], s[:, 2:3], s[:, 5:6]),
                            lambda e: e.tensor_scalar(s[:, 6:7], s[:, 6:7], -1.0,
                                                      None, ALU.mult),
                        ])
                    plan.op("vector", nbias, [wacr], [(dv, 1)])
                    r_fin[t] = dv.rec()
                    wnorm = [r_fin[t]]
                    if t >= 2:
                        if dst_image is None:
                            wnorm.append(r_stl[t - 2])
                        else:
                            wnorm.append(r_trl[t - 2])
                    plan.op("scalar",
                            lambda eng, xb=xb, bi=bi: eng.activation(
                                s_xh[bi][:], s_x[xb][:], AF.Identity,
                                bias=s_st2[bi][:, 6:7], scale=s_st2[bi][:, 5:6]),
                            wnorm, [(ac, 1)])
                    r_norm[t] = ac.rec()
                    if dst_image is None:
                        r_stl[t] = ST.issue(
                            lambda eng, t=t, bi=bi: eng.dma_start(
                                out=xh_plain[t * 128:(t + 1) * 128, :], in_=s_xh[bi][:]),
                            [r_norm[t]])
                    else:
                        for grpi in range(4):
                            w2 = [r_norm[t]] if grpi == 0 else []
                            if t >= 1:
                                w2.append(r_evl[t - 1])
                            if t >= 2 and grpi == 0:
                                w2.append(r_stl[t - 2])
                            for j in range(8):
                                kc = grpi * 8 + j
                                inc = [(pe, 1)] if j == 7 else []
                                plan.op("tensor",
                                        lambda eng, bi=bi, kc=kc, grpi=grpi, j=j:
                                        eng.transpose(
                                            p_tr[grpi][:, j * 128:(j + 1) * 128],
                                            s_xh[bi][:, kc * 128:(kc + 1) * 128],
                                            c_idb[:]),
                                        w2 if j == 0 else (), inc)
                            tok = pe.rec()
                            if grpi == 3:
                                r_trl[t] = tok
                            plan.op("vector",
                                    lambda eng, bi=bi, grpi=grpi: eng.tensor_copy(
                                        s_wtile[bi][:, grpi * 1024:(grpi + 1) * 1024],
                                        p_tr[grpi][:, :]),
                                    [tok], [(dv, 1)])
                        r_evl[t] = dv.rec()
                        r_stl[t] = ST.issue(
                            lambda eng, t=t, bi=bi: eng.dma_start(
                                out=dst_image[t], in_=s_wtile[bi][:]),
                            [r_evl[t]])
                fence()

        def plan_ffn(xh_image, res_dram, out_dram, n_tiles, tag=""):
            with ExitStack() as ph:
                PE_ = ph.enter_context
                s_wth = PE_(nc.sbuf_tensor("ffn_wt" + tag, [128, NT * 2048], BF16))
                c_bh = PE_(nc.sbuf_tensor("ffn_bh" + tag, [1, D], BF16))
                c_bl = PE_(nc.sbuf_tensor("ffn_bl" + tag, [1, D], BF16))
                _b32s = ExitStack()
                c_b32 = _b32s.enter_context(nc.sbuf_tensor("ffn_b32" + tag, [1, D], F32))
                t_b = LD.issue(lambda eng: eng.dma_start(out=c_b32[:], in_=fbias))

                def _bsplit(eng):
                    return chained(eng, CDV, [
                        lambda e: e.tensor_copy(c_bh[:], c_b32[:]),
                        lambda e: e.tensor_sub(c_bl[:], c_b32[:], c_bh[:]),
                    ])
                plan.op("vector", _bsplit, [t_b], [(dv, 1)])
                _bias_tok = dv.rec()
                _b32s.close()
                s_xt = [PE_(nc.sbuf_tensor(f"ffn_xt{tag}{i}", [128, D], BF16))
                        for i in range(2)]
                s_res = [PE_(nc.sbuf_tensor(f"ffn_r{tag}{i}", [128, 2048], F32))
                         for i in range(2)]
                s_o = [PE_(nc.sbuf_tensor(f"ffn_o{tag}{i}", [128, 2048], F32))
                       for i in range(2)]
                p_acc = [PE_(nc.psum_tensor(f"ffn_p{tag}{i}", [128, 2048], F32))
                         for i in range(2)]

                r_mm = {}
                r_epi = {}
                r_sto = {}
                for half in range(2):
                    ww = [r_mm[n_tiles - 1]] if half == 1 else []
                    wt_toks = []
                    for part in range(4):
                        src = d_wt.rearrange("(kc p) n -> p kc n", p=128)[
                            :, part * 8:(part + 1) * 8, half * 2048:(half + 1) * 2048]
                        dstv = s_wth[:].rearrange("p (kc n) -> p kc n", n=2048)[
                            :, part * 8:(part + 1) * 8, :]
                        wt_toks.append(LD.issue(
                            lambda eng, src=src, dstv=dstv: eng.dma_start(
                                out=dstv, in_=src),
                            ww if part == 0 else []))
                    for t in range(n_tiles):
                        gi = half * n_tiles + t
                        bi = gi % 2
                        xb = gi % 2
                        wl = list(wt_toks) if t == 0 else []
                        if gi == 0:
                            wl.append(_bias_tok)
                        if gi >= 2:
                            wl.append(r_mm[gi - 2])
                        t_xt = LD.issue(
                            lambda eng, t=t, xb=xb: eng.dma_start(
                                out=s_xt[xb][:], in_=xh_image[t]),
                            wl)
                        wl2 = [r_epi[gi - 2]] if gi >= 2 else []
                        if gi <= 1:
                            wl2.append(_bias_tok)
                        t_res = LD.issue(
                            lambda eng, t=t, xb=xb, half=half: eng.dma_start(
                                out=s_res[xb][:],
                                in_=res_dram[t * 128:(t + 1) * 128,
                                             half * 2048:(half + 1) * 2048]),
                            wl2)
                        wseed = [t_xt, (gp, 3), _bias_tok]
                        if gi >= 2:
                            wseed.append(r_epi[gi - 2])

                        def seeds(eng, bi=bi, half=half):
                            last = None
                            for nb4 in range(4):
                                cols = slice(half * 2048 + nb4 * 512,
                                             half * 2048 + nb4 * 512 + 512)
                                pcol = slice(nb4 * 512, nb4 * 512 + 512)
                                eng.matmul(p_acc[bi][:, pcol], c_ones1[:], c_bh[:, cols],
                                           start=True, stop=False)
                                last = eng.matmul(p_acc[bi][:, pcol], c_ones1[:],
                                                  c_bl[:, cols], start=False, stop=False)
                            return last
                        plan.op("tensor", seeds, wseed, ())

                        def mms(eng, bi=bi, xb=xb):
                            last = None
                            for kc in range(32):
                                for nb4 in range(4):
                                    pcol = slice(nb4 * 512, nb4 * 512 + 512)
                                    wcol = slice(kc * 2048 + nb4 * 512,
                                                 kc * 2048 + nb4 * 512 + 512)
                                    last = eng.matmul(
                                        p_acc[bi][:, pcol],
                                        s_xt[xb][:, kc * 128:(kc + 1) * 128],
                                        s_wth[:, wcol],
                                        start=False, stop=(kc == 31))
                            return last
                        plan.op("tensor", mms, [], [(pe, 1)])
                        r_mm[gi] = pe.rec()
                        wep = [r_mm[gi], t_res]
                        if gi >= 2:
                            wep.append(r_sto[gi - 2])
                        plan.op("vector",
                                lambda eng, bi=bi, xb=xb: eng.scalar_tensor_tensor(
                                    s_o[bi][:], p_acc[bi][:], 0.5, s_res[xb][:],
                                    ALU.mult, ALU.add),
                                wep, [(dv, 1)])
                        r_epi[gi] = dv.rec()
                        r_sto[gi] = ST.issue(
                            lambda eng, t=t, bi=bi, half=half: eng.dma_start(
                                out=out_dram[t * 128:(t + 1) * 128,
                                             half * 2048:(half + 1) * 2048],
                                in_=s_o[bi][:]),
                            [r_epi[gi]])
                fence()

        plan_ln(x_in, NT, d_x1h, tag="a")
        plan_ffn(d_x1h, x_in, d_x2, NT, tag="a")

        plan_ln(d_x2, NT, None, xh_plain=d_x2h, tag="b")

        with ExitStack() as ph:
            PE_ = ph.enter_context
            s_pan = PE_(nc.sbuf_tensor("p2_pan", [128, 32 * 2048], BF16))
            s_wb = [PE_(nc.sbuf_tensor(f"p2_wb{i}", [128, 32 * 128], BF16))
                    for i in range(2)]
            s_prj = [PE_(nc.sbuf_tensor(f"p2_prj{i}", [128, 2048], BF16))
                     for i in range(2)]
            s_t1 = PE_(nc.sbuf_tensor("p2_t1", [64, 1024], F32))
            s_t2 = PE_(nc.sbuf_tensor("p2_t2", [64, 1024], F32))
            s_csA = PE_(nc.sbuf_tensor("p2_csA", [128, 2048], F32))
            s_csB = PE_(nc.sbuf_tensor("p2_csB", [128, 2048], F32))
            s_pcp = PE_(nc.sbuf_tensor("p2_pcp", [128, 2048], F32))
            _p2p = PE_(nc.psum_tensor("p2_p0", [128, 2048], F32))
            p_acc = [_p2p, _p2p]

            r_mm2 = {}
            r_pcons = {}
            r_scons = {}
            r_ev2 = {}
            prev_rope_tok = None
            for th in range(2):
                chunks = [("k", i) for i in range(HK)] + [("v", i) for i in range(HK)]
                if th == 0:
                    chunks += [("q", i) for i in range(HQ)]
                wcs = []
                if th == 1:
                    wcs = [prev_rope_tok]
                cs_toks = [
                    LD.issue(lambda eng, th=th: eng.dma_start(
                        out=s_csA[0:64, :], in_=cosr[:, th * 2048:(th + 1) * 2048]), wcs),
                    LD.issue(lambda eng, th=th: eng.dma_start(
                        out=s_csA[64:128, :], in_=sinr[:, th * 2048:(th + 1) * 2048])),
                    LD.issue(lambda eng, th=th: eng.dma_start(
                        out=s_csB[0:64, :], in_=sinr[:, th * 2048:(th + 1) * 2048])),
                    LD.issue(lambda eng, th=th: eng.dma_start(
                        out=s_csB[64:128, :], in_=cosr[:, th * 2048:(th + 1) * 2048])),
                ]
                pan_toks = []
                for kc in range(32):
                    w = []
                    if th == 1 and kc == 0:
                        w = [r_mm2[(0, 47)]]
                    pan_toks.append(LD.issue(
                        lambda eng, kc=kc, th=th: eng.dma_start(
                            out=s_pan[:, kc * 2048:(kc + 1) * 2048],
                            in_=d_x2h[th * 2048:(th + 1) * 2048,
                                      kc * 128:(kc + 1) * 128],
                            transpose=True),
                        w))
                for ci, (kind, hh) in enumerate(chunks):
                    bi = ci % 2
                    wsrc = {"k": wk, "v": wv, "q": wq}[kind]
                    w = [r_mm2[(th, ci - 2)]] if ci >= 2 else (
                        [r_mm2[(0, 47)]] if th == 1 else [])
                    wb_toks = []
                    for part in range(4):
                        src = wsrc.rearrange("(kc p) n -> p kc n", p=128)[
                            :, part * 8:(part + 1) * 8, hh * 128:(hh + 1) * 128]
                        dstv = s_wb[bi][:].rearrange("p (kc n) -> p kc n", n=128)[
                            :, part * 8:(part + 1) * 8, :]
                        wb_toks.append(GD.issue(
                            lambda eng, src=src, dstv=dstv: eng.dma_start(
                                out=dstv, in_=src),
                            w if part == 0 else []))
                    wmm = wb_toks + pan_toks
                    if ci >= 1:
                        wmm.append(r_pcons[(th, ci - 1)])
                    elif th == 1:
                        wmm.append(r_pcons[(0, 47)])

                    def mms(eng, bi=bi):
                        last = None
                        for kc in range(32):
                            for nb4 in range(4):
                                pcol = slice(nb4 * 512, nb4 * 512 + 512)
                                last = eng.matmul(
                                    p_acc[bi][:, pcol],
                                    s_wb[bi][:, kc * 128:(kc + 1) * 128],
                                    s_pan[:, kc * 2048 + nb4 * 512:
                                          kc * 2048 + nb4 * 512 + 512],
                                    start=(kc == 0), stop=(kc == 31))
                        return last
                    plan.op("tensor", mms, wmm, [(pe, 1)])
                    r_mm2[(th, ci)] = pe.rec()

                    if kind in ("k", "q"):
                        plan.op("vector",
                                lambda eng, bi=bi: eng.tensor_copy(
                                    s_pcp[:], p_acc[bi][:]),
                                [r_mm2[(th, ci)]], [(dv, 1)])
                        r_pcons[(th, ci)] = dv.rec()

                        def rope(eng, bi=bi):
                            pa = s_pcp
                            steps = []
                            for q2 in range(2):
                                cs = slice(q2 * 1024, (q2 + 1) * 1024)
                                steps += [
                                    lambda e, cs=cs: e.tensor_mul(
                                        s_t1[:], pa[0:64, cs], s_csA[0:64, cs]),
                                    lambda e, cs=cs: e.tensor_mul(
                                        s_t2[:], pa[64:128, cs], s_csA[64:128, cs]),
                                    lambda e, cs=cs, bi=bi: e.tensor_sub(
                                        s_prj[bi][0:64, cs], s_t1[:], s_t2[:]),
                                    lambda e, cs=cs: e.tensor_mul(
                                        s_t1[:], pa[0:64, cs], s_csB[0:64, cs]),
                                    lambda e, cs=cs: e.tensor_mul(
                                        s_t2[:], pa[64:128, cs], s_csB[64:128, cs]),
                                    lambda e, cs=cs, bi=bi: e.tensor_add(
                                        s_prj[bi][64:128, cs], s_t1[:], s_t2[:]),
                                ]
                            return chained(eng, CDV, steps)
                        wr = list(cs_toks)
                        if ci >= 2:
                            wr.append(r_scons[(th, ci - 2)])
                        elif th == 1:
                            wr.append(r_scons[(0, 46 + ci)])
                        plan.op("vector", rope, wr, [(dv, 1)])
                        tokr = dv.rec()
                        prev_rope_tok = tokr
                        dst = d_k[hh, :, th * 2048:(th + 1) * 2048] if kind == "k" \
                            else d_q[hh]
                        r_scons[(th, ci)] = ST.issue(
                            lambda eng, dst=dst, bi=bi: eng.dma_start(
                                out=dst, in_=s_prj[bi][:]),
                            [tokr])
                    else:
                        wr = [r_mm2[(th, ci)]]
                        if ci >= 2:
                            wr.append(r_scons[(th, ci - 2)])
                        elif th == 1:
                            wr.append(r_scons[(0, 46 + ci)])
                        plan.op("vector",
                                lambda eng, bi=bi: eng.tensor_copy(
                                    s_prj[bi][:], p_acc[bi][:]),
                                wr, [(dv, 1)])
                        wev = dv.rec()
                        r_pcons[(th, ci)] = wev
                        dst = d_v[hh][:, th * 2048:(th + 1) * 2048]
                        r_scons[(th, ci)] = ST.issue(
                            lambda eng, dst=dst, bi=bi: eng.dma_start(
                                out=dst, in_=s_prj[bi][:]),
                            [wev])
            fence()

        with ExitStack() as ph:
            PE_ = ph.enter_context
            s_k = [PE_(nc.sbuf_tensor(f"a_k{i}", [128, TLOC], BF16)) for i in range(2)]
            s_v = [PE_(nc.sbuf_tensor(f"a_v{i}", [128, NT * 128], BF16))
                   for i in range(2)]
            s_q = [PE_(nc.sbuf_tensor(f"a_q{i}", [128, 4 * TOWN], BF16))
                   for i in range(2)]
            s_pt = PE_(nc.sbuf_tensor("a_pt", [128, 8 * 512], BF16))
            s_m1 = [PE_(nc.sbuf_tensor(f"a_m1{i}", [128, 64], F32)) for i in range(2)]
            s_mcol = [PE_(nc.sbuf_tensor(f"a_mcol{i}", [128, 1], F32)) for i in range(2)]
            s_bias = [PE_(nc.sbuf_tensor(f"a_bias{i}", [128, 1], F32)) for i in range(2)]
            s_mred = [PE_(nc.sbuf_tensor(f"a_mred{i}", [1, 1], F32)) for i in range(2)]
            s_sumr = PE_(nc.sbuf_tensor("a_sumr", [1, 512], F32))
            s_rb = PE_(nc.sbuf_tensor("a_rb", [128, 512], F32))
            s_ot = PE_(nc.sbuf_tensor("a_ot", [128, 512], BF16))
            RING = 40
            s_ring = PE_(nc.sbuf_tensor("a_ring", [128, RING * 512], F32))
            p_s2 = [PE_(nc.psum_tensor(f"a_s2{i}", [128, 512], F32)) for i in range(5)]
            p_o = PE_(nc.psum_tensor("a_o", [128, 512], F32))
            p_sum = PE_(nc.psum_tensor("a_sum", [128, 512], F32))
            p_scr = PE_(nc.psum_tensor("a_scr", [128, 512], F32))

            def ktiles(qb):
                out = []
                for sp in range(4 * qb + 4):
                    for side in range(2):
                        col = sp + 16 * side
                        delta = sp - 4 * qb
                        r0 = max(0, delta) * 128
                        msk = None
                        if delta >= 0:
                            msk = c_mtri if side == 0 else c_mpar
                        out.append((col, r0, msk))
                return out

            LOOKAHEAD = 6
            items = []
            for g in range(HK):
                for hh in range(4):
                    for qb in range(4):
                        items.append((g, hh, qb))
            NITEM = len(items)

            g_toks = {}
            r_gend = {}
            r_copy = {}
            r_mskd = {}
            r_exp = {}
            r_avt = {}
            r_ptfree = {}
            it_state = {}
            kg1 = [0]
            kg2 = [0]
            prev_store = [None]
            prev_evo = [None]

            def emit_loads(g):
                w = [r_gend[g - 2]] if g >= 2 else []
                toks = [LD.issue(lambda eng, g=g: eng.dma_start(
                    out=s_k[g % 2][:], in_=d_k[g]), w)]
                for vt in range(NT):
                    toks.append(LD.issue(
                        lambda eng, g=g, vt=vt: eng.dma_start(
                            out=s_v[g % 2][:, vt * 128:(vt + 1) * 128],
                            in_=d_v[g][:, vt * 128:(vt + 1) * 128],
                            transpose=True)))
                for hh in range(4):
                    toks.append(LD.issue(
                        lambda eng, g=g, hh=hh: eng.dma_start(
                            out=s_q[g % 2][:, hh * TOWN:(hh + 1) * TOWN],
                            in_=d_q[4 * g + hh])))
                g_toks[g] = toks

            def gen_p1(m):
                g, hh, qb = items[m]
                gb = g % 2
                mb = m % 2
                kts = ktiles(qb)
                it_state[m] = {"kts": kts}
                first_of_g = (hh == 0 and qb == 0)
                lastred = None
                for i, (col, r0, msk) in enumerate(kts):
                    kg = kg1[0]
                    sb = kg % 5
                    slot = kg % RING
                    w1 = []
                    if first_of_g and i == 0:
                        w1 += g_toks[g]
                    if kg >= 5:
                        w1.append(r_copy[kg - 5])
                    plan.op("tensor",
                            lambda eng, gb=gb, col=col, sb=sb, hh=hh, qb=qb:
                            eng.matmul(
                                p_s2[sb][:, 0:512],
                                s_k[gb][:, col * 128:(col + 1) * 128],
                                s_q[gb][:, hh * TOWN + qb * 512:
                                        hh * TOWN + qb * 512 + 512],
                                start=True, stop=True),
                            w1, [(pe, 1)])
                    tokm = pe.rec()
                    rc = 0 if kg < RING else r0
                    wc = [tokm]
                    if kg >= RING:
                        wc.append(r_exp[kg - RING])
                    plan.op("scalar",
                            lambda eng, sb=sb, slot=slot, rc=rc: eng.activation(
                                s_ring[:, slot * 512 + rc:slot * 512 + 512],
                                p_s2[sb][:, rc:512], AF.Identity),
                            wc, [(ac, 1)])
                    r_copy[kg] = ac.rec()
                    if msk is not None:
                        plan.op("vector",
                                lambda eng, slot=slot, r0=r0, msk=msk:
                                eng.tensor_add(
                                    s_ring[:, slot * 512 + r0:
                                           slot * 512 + r0 + 128],
                                    s_ring[:, slot * 512 + r0:
                                           slot * 512 + r0 + 128], msk[:]),
                                [r_copy[kg]], [(dv, 1)])
                        r_mskd[kg] = dv.rec()
                    if i % 4 == 3:
                        pslot = (kg - 3) % RING
                        plan.op("vector",
                                lambda eng, i=i, mb=mb, pslot=pslot:
                                eng.tensor_reduce(
                                    s_m1[mb][:, i // 4:i // 4 + 1],
                                    s_ring[:, pslot * 512:pslot * 512 + 2048],
                                    AX.X, ALU.max),
                                [r_copy[kg]], [(dv, 1)])
                        lastred = dv.rec()
                    kg1[0] += 1
                    yield
                nk = len(kts)
                plan.op("vector",
                        lambda eng, nk=nk, mb=mb: eng.tensor_reduce(
                            s_mcol[mb][:], s_m1[mb][:, 0:nk // 4], AX.X, ALU.max),
                        [lastred], [(dv, 1)])
                it_state[m]["comb"] = dv.rec()

            def emit_chain(m):
                st_ = it_state[m]
                mb = m % 2
                wtr = [st_["comb"]]
                if prev_evo[0] is not None:
                    wtr.append(prev_evo[0])
                plan.op("tensor",
                        lambda eng, mb=mb: eng.transpose(
                            p_scr[0:1, 0:128], s_mcol[mb][:], c_idf[:]),
                        wtr, [(pe, 1)])
                tok2 = pe.rec()
                plan.op("vector",
                        lambda eng, mb=mb: eng.tensor_reduce(
                            s_mred[mb][:], p_scr[0:1, 0:128], AX.X, ALU.max),
                        [tok2], [(dv, 1)])
                tok3 = dv.rec()
                plan.op("tensor",
                        lambda eng, mb=mb: eng.matmul(
                            p_scr[:, 0:1], c_onesf[:], s_mred[mb][:],
                            start=True, stop=True),
                        [tok3], [(pe, 1)])
                tok4 = pe.rec()

                def mkbias(eng, mb=mb):
                    return chained(eng, CDV, [
                        lambda e: e.tensor_copy(s_mcol[mb][:], p_scr[:, 0:1]),
                        lambda e: e.tensor_scalar(s_bias[mb][:], s_mcol[mb][:],
                                                  -SCL, None, ALU.mult),
                    ])
                plan.op("vector", mkbias, [tok4], [(dv, 1)])
                st_["bias"] = dv.rec()

            def gen_p2(m):
                g, hh, qb = items[m]
                gb = g % 2
                mb = m % 2
                kts = it_state[m]["kts"]
                nk = len(kts)
                pend = []
                avq = []
                prev_ent = [None]

                def emit_av(ent, i_now):
                    ptslot, r0, kgx, li, col, last = ent
                    first = (li == 0)

                    def av(eng, gb=gb, col=col, r0=r0, ptslot=ptslot,
                           first=first, last=last):
                        return eng.matmul(p_o[:, r0:512],
                                          s_v[gb][:, col * 128:(col + 1) * 128],
                                          s_pt[:, ptslot * 512 + r0:
                                               ptslot * 512 + 512],
                                          start=first, stop=last)
                    wav_ = [r_exp[kgx]]
                    if first and prev_evo[0] is not None:
                        wav_.append(prev_evo[0])
                    plan.op("tensor", av, wav_, [(pe, 1)])
                    r_avt[kgx] = pe.rec()
                    pend.append((ptslot, r0, kgx, li))
                    if len(pend) == 3 or last:
                        def sums(eng, pend=tuple(pend), nk=nk):
                            lastins = None
                            for (sb_, r0_, _, lj) in pend:
                                j = lj % 3
                                lastins = eng.matmul(
                                    p_sum[32 * j:32 * j + 1, r0_:512],
                                    c_onescol[:],
                                    s_pt[:, sb_ * 512 + r0_:sb_ * 512 + 512],
                                    start=(lj < 3), stop=(lj + 3 >= nk),
                                    tile_position=(0, 32 * j))
                            return lastins
                        plan.op("tensor", sums, [r_exp[pend[-1][2]]], [(pe, 1)])
                        grp_tok = pe.rec()
                        for (_, _, kgx2, _) in pend:
                            r_ptfree[kgx2] = grp_tok
                        del pend[:]

                for i, (col, r0, msk) in enumerate(kts):
                    kg = kg2[0]
                    slot = kg % RING
                    ptslot = kg % 8
                    if i % 2 == 1:
                        eslot = (kg - 1) % RING
                        pslot = (kg - 1) % 8
                        wexp = []
                        if i == 1:
                            wexp.append(it_state[m]["bias"])
                        for kk in (kg - 1, kg):
                            if kk in r_mskd:
                                wexp.append(r_mskd[kk])
                        if kg >= 9:
                            wexp.append(r_ptfree[kg - 9])
                        if kg >= 8:
                            wexp.append(r_ptfree[kg - 8])
                        plan.op("scalar",
                                lambda eng, eslot=eslot, pslot=pslot, r0=r0,
                                mb=mb: eng.activation(
                                    s_pt[:, pslot * 512 + r0:pslot * 512 + 1024],
                                    s_ring[:, eslot * 512 + r0:
                                           eslot * 512 + 1024],
                                    AF.Exp, bias=s_bias[mb][:], scale=SCL),
                                wexp, [(ac, 1)])
                        r_exp[kg - 1] = r_exp[kg] = ac.rec()
                        avq.append(prev_ent[0])
                        avq.append((ptslot, r0, kg, i, col, i == nk - 1))
                        while len(avq) > 3:
                            emit_av(avq.pop(0), i)
                    else:
                        prev_ent[0] = (ptslot, r0, kg, i, col, i == nk - 1)
                    kg2[0] += 1
                    yield
                while avq:
                    emit_av(avq.pop(0), nk - 1)
                wav = r_avt[kg2[0] - 1]
                minr0 = [512, 512, 512]
                for li, (_, r0_, _) in enumerate(kts):
                    j = li % 3
                    minr0[j] = min(minr0[j], r0_)

                def sums_fin(eng, minr0=tuple(minr0)):
                    steps = [lambda e: e.memset(s_sumr[:], 0.0)]
                    for j in range(3):
                        if minr0[j] < 512:
                            steps.append(
                                lambda e, j=j, a=minr0[j]: e.scalar_tensor_tensor(
                                    s_sumr[:, a:512], p_sum[32 * j:32 * j + 1, a:512],
                                    1.0, s_sumr[:, a:512], ALU.mult, ALU.add))
                    steps += [
                        lambda e: e.tensor_scalar(s_sumr[:], s_sumr[:], 1e-30,
                                                  None, ALU.add),
                        lambda e: e.reciprocal(s_sumr[:], s_sumr[:]),
                    ]
                    return chained(eng, CDV, steps)
                plan.op("vector", sums_fin, [wav], [(dv, 1)])
                tok5 = dv.rec()
                plan.op("tensor",
                        lambda eng: eng.matmul(p_scr[:], c_onesf[:], s_sumr[:],
                                               start=True, stop=True),
                        [tok5], [(pe, 1)])
                tok6 = pe.rec()

                def evo(eng):
                    return chained(eng, CDV, [
                        lambda e: e.tensor_copy(s_rb[:], p_scr[:]),
                        lambda e: e.tensor_mul(s_ot[:], p_o[:], s_rb[:]),
                    ])
                wevo = [tok6]
                if prev_store[0] is not None:
                    wevo.append(prev_store[0])
                plan.op("vector", evo, wevo, [(dv, 1)])
                prev_evo[0] = dv.rec()
                dst = d_ot[(4 * g + hh) * 128:(4 * g + hh + 1) * 128,
                           qb * 512:(qb + 1) * 512]
                prev_store[0] = ST.issue(
                    lambda eng, dst=dst: eng.dma_start(out=dst, in_=s_ot[:]),
                    [prev_evo[0]])
                if hh == 3 and qb == 3:
                    r_gend[g] = pe.rec()
                yield

            def drain(gen):
                for _ in gen:
                    pass

            emit_loads(0)
            emit_loads(1)
            drain(gen_p1(0))
            for m in range(NITEM):
                g, hh, qb = items[m]
                if hh == 0 and qb == 1 and (g + 1) < HK and (g + 1) not in g_toks:
                    emit_loads(g + 1)
                A = gen_p1(m + 1) if m + 1 < NITEM else iter(())
                for _ in range(LOOKAHEAD):
                    if next(A, "end") == "end":
                        break
                emit_chain(m)
                B = gen_p2(m)
                alive = True
                while alive:
                    alive = False
                    if next(A, "end") != "end":
                        alive = True
                    if next(B, "end") != "end":
                        alive = True
            fence()

        with ExitStack() as ph:
            PE_ = ph.enter_context
            s_wo = PE_(nc.sbuf_tensor("w_wo", [128, 32 * 2048], BF16))
            s_otl = [PE_(nc.sbuf_tensor(f"w_ot{i}", [128, D], BF16)) for i in range(2)]
            s_res = [PE_(nc.sbuf_tensor(f"w_r{i}", [128, 2048], F32)) for i in range(2)]
            s_o = [PE_(nc.sbuf_tensor(f"w_o{i}", [128, 2048], F32)) for i in range(2)]
            p_acc = [PE_(nc.psum_tensor(f"w_p{i}", [128, 2048], F32)) for i in range(2)]

            r_mm5 = {}
            r_epi5 = {}
            r_sto5 = {}
            for half in range(2):
                ww = [r_mm5[NTOWN - 1]] if half == 1 else []
                wo_toks = []
                for cc in range(32):
                    src = wo.rearrange("(cc p) n -> cc p n", p=128)[
                        cc, :, half * 2048:(half + 1) * 2048]
                    wo_toks.append(GD.issue(
                        lambda eng, src=src, cc=cc: eng.dma_start(
                            out=s_wo[:, cc * 2048:(cc + 1) * 2048], in_=src),
                        ww if cc == 0 else []))
                for t in range(NTOWN):
                    gi = half * NTOWN + t
                    bi = gi % 2
                    xb = gi % 2
                    wl = list(wo_toks) if t == 0 else []
                    if gi >= 2:
                        wl.append(r_mm5[gi - 2])
                    ot_toks = []
                    for part in range(4):
                        src = d_ot.rearrange("(cc p) (t n) -> cc p t n", p=128, n=128)[
                            part * 8:(part + 1) * 8, :, t, :].rearrange(
                            "cc p n -> p cc n")
                        dstv = s_otl[xb][:].rearrange("p (cc n) -> p cc n", n=128)[
                            :, part * 8:(part + 1) * 8, :]
                        ot_toks.append(LD.issue(
                            lambda eng, src=src, dstv=dstv: eng.dma_start(
                                out=dstv, in_=src),
                            wl if part == 0 else []))
                    wl2 = [r_epi5[gi - 2]] if gi >= 2 else []
                    t_res = LD.issue(
                        lambda eng, t=t, xb=xb, half=half: eng.dma_start(
                            out=s_res[xb][:],
                            in_=d_x2[t * 128:(t + 1) * 128,
                                     half * 2048:(half + 1) * 2048]),
                        wl2)
                    wmm = list(ot_toks)
                    if gi >= 2:
                        wmm.append(r_epi5[gi - 2])

                    def mms(eng, bi=bi, xb=xb):
                        last = None
                        for cc in range(32):
                            for nb4 in range(4):
                                pcol = slice(nb4 * 512, nb4 * 512 + 512)
                                wcol = slice(cc * 2048 + nb4 * 512,
                                             cc * 2048 + nb4 * 512 + 512)
                                last = eng.matmul(
                                    p_acc[bi][:, pcol],
                                    s_otl[xb][:, cc * 128:(cc + 1) * 128],
                                    s_wo[:, wcol],
                                    start=(cc == 0), stop=(cc == 31))
                        return last
                    plan.op("tensor", mms, wmm, [(pe, 1)])
                    r_mm5[gi] = pe.rec()
                    wep = [r_mm5[gi], t_res]
                    if gi >= 2:
                        wep.append(r_sto5[gi - 2])
                    plan.op("vector",
                            lambda eng, bi=bi, xb=xb: eng.tensor_add(
                                s_o[bi][:], p_acc[bi][:], s_res[xb][:]),
                            wep, [(dv, 1)])
                    r_epi5[gi] = dv.rec()
                    r_sto5[gi] = ST.issue(
                        lambda eng, t=t, bi=bi, half=half: eng.dma_start(
                            out=d_x3[t * 128:(t + 1) * 128,
                                     half * 2048:(half + 1) * 2048],
                            in_=s_o[bi][:]),
                        [r_epi5[gi]])
            fence()

        plan_ln(d_x3, NTOWN, d_x3h, tag="c")
        plan_ffn(d_x3h, d_x3, o_out, NTOWN, tag="c")

        plan.wait_all("sync", SEMS)

        with nc.Block() as blk:
            @blk.sync
            def _(eng):
                for f in plan.q["sync"]:
                    f(eng)

            @blk.gpsimd
            def _(eng):
                for f in plan.q["gpsimd"]:
                    f(eng)

            @blk.tensor
            def _(eng):
                for f in plan.q["tensor"]:
                    f(eng)

            @blk.vector
            def _(eng):
                for f in plan.q["vector"]:
                    f(eng)

            @blk.scalar
            def _(eng):
                for f in plan.q["scalar"]:
                    f(eng)

    return nc


def _host_inputs(inputs):
    X = np.asarray(inputs["L0_attn_X"])
    Wq = np.ascontiguousarray(np.asarray(inputs["L0_attn_Wq"]))
    Wk = np.ascontiguousarray(np.asarray(inputs["L0_attn_Wk"]))
    Wv = np.ascontiguousarray(np.asarray(inputs["L0_attn_Wv"]))
    Wo = np.ascontiguousarray(np.asarray(inputs["L0_attn_Wo"]))
    Wqi = np.ascontiguousarray(np.asarray(inputs["L0_ffn_weight_q"]))
    Zp = np.ascontiguousarray(np.asarray(inputs["L0_ffn_zp"]))
    Sc = np.ascontiguousarray(np.asarray(inputs["L0_ffn_scale"]))
    Bias = np.ascontiguousarray(np.asarray(inputs["L0_ffn_bias"]).reshape(1, D))
    ident = np.eye(128, dtype=np.float32)

    inv_freq = (1.0 / (10000.0 ** (np.arange(0, 64, dtype=np.float32) / 64.0))) \
        .astype(np.float32)
    tri = np.where(np.arange(128)[:, None] <= np.arange(128)[None, :], 0.0, NEG) \
        .astype(np.float32)

    in_maps = []
    perms = []
    for c in range(8):
        b, p = c // 2, c % 2
        own = np.concatenate([np.arange(128 * (2 * s + p), 128 * (2 * s + p) + 128)
                              for s in range(16)])
        oth = np.concatenate([np.arange(128 * (2 * s + 1 - p),
                                        128 * (2 * s + 1 - p) + 128)
                              for s in range(16)])
        perm = np.concatenate([own, oth])
        perms.append(own)
        pos = perm.astype(np.float32)
        ang = pos[None, :] * inv_freq[:, None]
        cosr_ = np.cos(ang).astype(np.float32)
        sinr_ = np.sin(ang).astype(np.float32)
        mpar = np.zeros((128, 128), np.float32) if p == 1 \
            else np.full((128, 128), NEG, np.float32)
        in_maps.append(dict(
            x_in=np.ascontiguousarray(X[b][perm]),
            wq=Wq, wk=Wk, wv=Wv, wo=Wo, wqint=Wqi, zp=Zp, qsc=Sc, fbias=Bias,
            cosr=cosr_, sinr=sinr_, mtri=tri, mpar=mpar, identf=ident,
        ))
    return in_maps, perms


def kernel(**inputs):
    in_maps, perms = _host_inputs(inputs)
    nc = build_nc()
    res = run_bass_kernel_spmd(nc, in_maps, core_ids=list(range(8)))
    out = np.empty((4, 4096, 4096), dtype=np.float32)
    for c in range(8):
        b = c // 2
        out[b][perms[c]] = res.results[c]["o_out"]
    return out


if __name__ == "__main__":
    import reference as _ref
    ins = _ref.setup_inputs(0)
    got = kernel(**{k: np.asarray(v) for k, v in ins.items()})
    exp = np.asarray(_ref.reference(**ins))
    num = float(np.mean((got - exp) ** 2))
    den = float(np.mean(exp ** 2))
    print("resvar:", num / den)


# version 16
# speedup vs baseline: 1.0451x; 1.0165x over previous
import numpy as np
import ml_dtypes
from contextlib import ExitStack

import concourse.bass as bass
import concourse.mybir as mybir
from concourse.bass_utils import run_bass_kernel_spmd

F32 = mybir.dt.float32
BF16 = mybir.dt.bfloat16
I32 = mybir.dt.int32
AF = mybir.ActivationFunctionType
ALU = mybir.AluOpType
AX = mybir.AxisListType

D = 4096
NT = 32
NTOWN = 16
TLOC = NT * 128
TOWN = NTOWN * 128
HK = 8
HQ = 32
NGRP = 64
EPS = 1e-5
SCL = float(1.0 / np.sqrt(128.0))
NEG = -1.0e30

ENGINES = ("sync", "gpsimd", "tensor", "vector", "scalar")


class S:
    __slots__ = ("h", "n", "name")

    def __init__(self, h, name):
        self.h = h
        self.n = 0
        self.name = name

    def rec(self):
        return (self, self.n)


def _mkop(waits, fn, incs):
    def run(eng):
        for (h, v) in waits:
            eng.wait_ge(h, v)
        ins = fn(eng)
        for (h, k) in incs:
            ins.then_inc(h, k)
    return run


class Plan:
    def __init__(self):
        self.q = {e: [] for e in ENGINES}
        self.selfsem = {}

    def op(self, e, fn, waits=(), incs=()):
        waits = list(waits)
        ss = self.selfsem.get(e)
        if ss is not None and ss.n > 0:
            waits.append((ss, ss.n))
        if ss is not None and not incs:
            incs = [(ss, 1)]
        w = tuple((s.h, int(v)) for (s, v) in waits if v > 0)
        ii = []
        for (s, k) in incs:
            s.n += k
            ii.append((s.h, k))
        self.q[e].append(_mkop(w, fn, tuple(ii)))

    def wait_all(self, e, sems):
        w = tuple((s.h, s.n) for s in sems if s.n > 0)

        def run(eng):
            for (h, v) in w:
                eng.wait_ge(h, v)
        self.q[e].append(run)


class CC:

    def __init__(self, h):
        self.h = h
        self.n = 0

    def link(self, ins):
        ins.then_inc(self.h, 1)
        self.n += 1

    def step(self, eng, ins, last=False):
        self.link(ins)

    def barrier(self, eng):
        eng.wait_ge(self.h, self.n)


def chained(eng, cc, steps):
    last = None
    for k, f in enumerate(steps):
        if k > 0:
            cc.link(last)
            cc.barrier(eng)
        last = f(eng)
    return last


class Chan:

    def __init__(self, plan, eng, sems):
        self.plan = plan
        self.eng = eng
        self.sems = sems
        self.i = 0

    def issue(self, fn, waits=()):
        s = self.sems[self.i % len(self.sems)]
        w = [(s, s.n)] + list(waits)
        self.plan.op(self.eng, fn, w, [(s, 16)])
        self.i += 1
        return (s, s.n)


def build_nc():
    nc = bass.Bass("TRN2", target_bir_lowering=False, debug=False)

    x_in = nc.dram_tensor("x_in", [TLOC, D], F32, kind="ExternalInput").ap()
    wq = nc.dram_tensor("wq", [D, D], F32, kind="ExternalInput").ap()
    wk = nc.dram_tensor("wk", [D, HK * 128], F32, kind="ExternalInput").ap()
    wv = nc.dram_tensor("wv", [D, HK * 128], F32, kind="ExternalInput").ap()
    wo = nc.dram_tensor("wo", [D, D], F32, kind="ExternalInput").ap()
    wqint = nc.dram_tensor("wqint", [D, D], I32, kind="ExternalInput").ap()
    zp = nc.dram_tensor("zp", [D, NGRP], F32, kind="ExternalInput").ap()
    qsc = nc.dram_tensor("qsc", [D, NGRP], F32, kind="ExternalInput").ap()
    fbias = nc.dram_tensor("fbias", [1, D], F32, kind="ExternalInput").ap()
    cosr = nc.dram_tensor("cosr", [64, TLOC], F32, kind="ExternalInput").ap()
    sinr = nc.dram_tensor("sinr", [64, TLOC], F32, kind="ExternalInput").ap()
    mtri = nc.dram_tensor("mtri", [128, 128], F32, kind="ExternalInput").ap()
    mpar = nc.dram_tensor("mpar", [128, 128], F32, kind="ExternalInput").ap()
    identf = nc.dram_tensor("identf", [128, 128], F32, kind="ExternalInput").ap()

    o_out = nc.dram_tensor("o_out", [TOWN, D], F32, kind="ExternalOutput").ap()

    d_wt = nc.dram_tensor("d_wt", [D, D], BF16, kind="Internal").ap()
    d_x1h = nc.dram_tensor("d_x1h", [NT, 128, D], BF16, kind="Internal").ap()
    d_x2 = nc.dram_tensor("d_x2", [TLOC, D], F32, kind="Internal").ap()
    d_x2h = nc.dram_tensor("d_x2h", [TLOC, D], BF16, kind="Internal").ap()
    d_k = nc.dram_tensor("d_k", [HK, 128, TLOC], BF16, kind="Internal").ap()
    d_v = nc.dram_tensor("d_v", [HK, 128, TLOC], BF16, kind="Internal").ap()
    d_q = nc.dram_tensor("d_q", [HQ, 128, TOWN], BF16, kind="Internal").ap()
    d_ot = nc.dram_tensor("d_ot", [D, TOWN], BF16, kind="Internal").ap()
    d_x3 = nc.dram_tensor("d_x3", [TOWN, D], F32, kind="Internal").ap()
    d_x3h = nc.dram_tensor("d_x3h", [NTOWN, 128, D], BF16, kind="Internal").ap()

    plan = Plan()

    with ExitStack() as top:
        E = top.enter_context
        pe = S(E(nc.semaphore("s_pe")), "pe")
        dv = S(E(nc.semaphore("s_dv")), "dv")
        ac = S(E(nc.semaphore("s_ac")), "ac")
        gp = S(E(nc.semaphore("s_gp")), "gp")
        lds = [S(E(nc.semaphore(f"s_ld{i}")), f"ld{i}") for i in range(6)]
        gds = [S(E(nc.semaphore(f"s_gd{i}")), f"gd{i}") for i in range(4)]
        sts = [S(E(nc.semaphore(f"s_st{i}")), f"st{i}") for i in range(4)]
        SEMS = tuple([pe, dv, ac, gp] + lds + gds + sts)

        plan.selfsem = {"vector": dv, "scalar": ac, "tensor": pe, "gpsimd": gp}
        CDV = CC(E(nc.semaphore("s_cdv")))
        CAC = CC(E(nc.semaphore("s_cac")))
        LD = Chan(plan, "sync", lds)
        GD = Chan(plan, "gpsimd", gds)
        ST = Chan(plan, "sync", sts)

        def fence():
            for e in ENGINES:
                plan.wait_all(e, SEMS)

        c_ones1 = E(nc.sbuf_tensor("c_ones1", [1, 128], BF16))
        c_onesf = E(nc.sbuf_tensor("c_onesf", [1, 128], F32))
        c_onescol = E(nc.sbuf_tensor("c_onescol", [128, 1], BF16))
        c_idb = E(nc.sbuf_tensor("c_idb", [128, 128], BF16))
        c_idf = E(nc.sbuf_tensor("c_idf", [128, 128], F32))
        c_mtri = E(nc.sbuf_tensor("c_mtri", [128, 128], F32))
        c_mpar = E(nc.sbuf_tensor("c_mpar", [128, 128], F32))

        LD.issue(lambda eng: eng.dma_start(out=c_mtri[:], in_=mtri))
        LD.issue(lambda eng: eng.dma_start(out=c_mpar[:], in_=mpar))
        LD.issue(lambda eng: eng.dma_start(out=c_idf[:], in_=identf))
        GD.issue(lambda eng: eng.dma_start(out=c_idb[:], in_=identf))
        plan.op("gpsimd", lambda eng: eng.memset(c_ones1[:], 1.0), (), [(gp, 1)])
        plan.op("gpsimd", lambda eng: eng.memset(c_onesf[:], 1.0), (), [(gp, 1)])
        plan.op("gpsimd", lambda eng: eng.memset(c_onescol[:], 1.0), (), [(gp, 1)])
        fence()

        with ExitStack() as ph:
            PE_ = ph.enter_context
            s_wq = [PE_(nc.sbuf_tensor(f"p0_wq{i}", [128, D], BF16)) for i in range(2)]
            s_zp = [PE_(nc.sbuf_tensor(f"p0_zp{i}", [128, NGRP], F32)) for i in range(2)]
            s_sc = [PE_(nc.sbuf_tensor(f"p0_sc{i}", [128, NGRP], F32)) for i in range(2)]
            s_wd = [PE_(nc.sbuf_tensor(f"p0_wd{i}", [128, D], BF16)) for i in range(2)]
            s_wt = [PE_(nc.sbuf_tensor(f"p0_wt{i}", [128, D], BF16)) for i in range(2)]
            p_tr = [PE_(nc.psum_tensor(f"p0_ptr{i}", [128, 1024], BF16)) for i in range(4)]

            r_deq = {}
            r_tr = {}
            r_ev = {}
            r_st = {}
            for nb in range(NT):
                bi = nb % 2
                w_gd = [r_deq[nb - 2]] if nb >= 2 else []
                t_wq = GD.issue(
                    lambda eng, nb=nb, bi=bi: eng.dma_start(
                        out=s_wq[bi][:], in_=wqint[nb * 128:(nb + 1) * 128, :]),
                    w_gd)
                t_zp = LD.issue(
                    lambda eng, nb=nb, bi=bi: eng.dma_start(
                        out=s_zp[bi][:], in_=zp[nb * 128:(nb + 1) * 128, :]),
                    w_gd)
                t_sc = LD.issue(
                    lambda eng, nb=nb, bi=bi: eng.dma_start(
                        out=s_sc[bi][:], in_=qsc[nb * 128:(nb + 1) * 128, :]))
                w0 = [t_wq, t_zp, t_sc]
                if nb >= 2:
                    w0.append(r_tr[(nb - 2, 3)])
                for g in range(NGRP):
                    inc = [(dv, 1)] if g == NGRP - 1 else []
                    plan.op("vector",
                            lambda eng, bi=bi, g=g: eng.tensor_scalar(
                                s_wd[bi][:, g * 64:(g + 1) * 64],
                                s_wq[bi][:, g * 64:(g + 1) * 64],
                                s_zp[bi][:, g:g + 1], s_sc[bi][:, g:g + 1],
                                ALU.subtract, ALU.mult),
                            w0 if g == 0 else (), inc)
                r_deq[nb] = dv.rec()
                for grpi in range(4):
                    w = [r_deq[nb]] if grpi == 0 else []
                    if nb >= 1:
                        w.append(r_ev[nb - 1])
                    if nb >= 2 and grpi == 0:
                        w.extend(r_st[nb - 2])
                    for j in range(8):
                        kc = grpi * 8 + j
                        inc = [(pe, 1)] if j == 7 else []
                        plan.op("tensor",
                                lambda eng, bi=bi, kc=kc, grpi=grpi, j=j: eng.transpose(
                                    p_tr[grpi][:, j * 128:(j + 1) * 128],
                                    s_wd[bi][:, kc * 128:(kc + 1) * 128], c_idb[:]),
                                w if j == 0 else (), inc)
                    r_tr[(nb, grpi)] = pe.rec()
                    plan.op("vector",
                            lambda eng, bi=bi, grpi=grpi: eng.tensor_copy(
                                s_wt[bi][:, grpi * 1024:(grpi + 1) * 1024],
                                p_tr[grpi][:, :]),
                            [r_tr[(nb, grpi)]], [(dv, 1)])
                r_ev[nb] = dv.rec()
                stoks = []
                for part in range(4):
                    dst = d_wt.rearrange("(kc p) (nb n) -> p kc nb n", p=128, n=128)[
                        :, part * 8:(part + 1) * 8, nb, :]
                    srcv = s_wt[bi][:].rearrange("p (kc n) -> p kc n", n=128)[
                        :, part * 8:(part + 1) * 8, :]
                    stoks.append(ST.issue(
                        lambda eng, dst=dst, srcv=srcv: eng.dma_start(out=dst, in_=srcv),
                        [r_ev[nb]] if part == 0 else []))
                r_st[nb] = stoks
            fence()

        def plan_ln(src_dram, n_tiles, dst_image, xh_plain=None, tag=""):
            with ExitStack() as ph:
                PE_ = ph.enter_context
                s_x = [PE_(nc.sbuf_tensor(f"ln_x{tag}{i}", [128, D], F32))
                       for i in range(3)]
                s_scr = [PE_(nc.sbuf_tensor(f"ln_scr{tag}{i}", [128, D], F32))
                         for i in range(2)]
                s_xh = [PE_(nc.sbuf_tensor(f"ln_xh{tag}{i}", [128, D], BF16))
                        for i in range(2)]
                s_st2 = [PE_(nc.sbuf_tensor(f"ln_st{tag}{i}", [128, 8], F32))
                         for i in range(2)]
                if dst_image is not None:
                    p_tr = [PE_(nc.psum_tensor(f"ln_ptr{tag}{i}", [128, 1024], BF16))
                            for i in range(4)]
                    s_wtile = [PE_(nc.sbuf_tensor(f"ln_wt{tag}{i}", [128, D], BF16))
                               for i in range(2)]
                r_norm = {}
                r_trl = {}
                r_evl = {}
                r_stl = {}
                r_fin = {}
                for t in range(n_tiles):
                    bi = t % 2
                    xb = t % 3
                    w = [r_norm[t - 3]] if t >= 3 else []
                    t_x = LD.issue(
                        lambda eng, t=t, xb=xb: eng.dma_start(
                            out=s_x[xb][:], in_=src_dram[t * 128:(t + 1) * 128, :]),
                        w)
                    wstat = [t_x]
                    if t >= 2:
                        wstat.append(r_fin[t - 2])
                    plan.op("vector",
                            lambda eng, xb=xb, bi=bi: eng.tensor_reduce(
                                s_st2[bi][:, 0:1], s_x[xb][:], AX.X, ALU.add),
                            wstat, [(dv, 1)])
                    wsq = [t_x]
                    if t >= 2:
                        wsq.append(r_norm[t - 2])
                    plan.op("scalar",
                            lambda eng, xb=xb, bi=bi: eng.activation(
                                s_scr[bi][:], s_x[xb][:], AF.Square,
                                accum_out=s_st2[bi][:, 1:2]),
                            wsq, [(ac, 1)])
                    wac = ac.rec()

                    def fin(eng, bi=bi):
                        s = s_st2[bi]
                        return chained(eng, CDV, [
                            lambda e: e.tensor_scalar(s[:, 2:3], s[:, 0:1], 1.0 / D,
                                                      None, ALU.mult),
                            lambda e: e.tensor_scalar(s[:, 3:4], s[:, 1:2], 1.0 / D,
                                                      None, ALU.mult),
                            lambda e: e.tensor_mul(s[:, 7:8], s[:, 2:3], s[:, 2:3]),
                            lambda e: e.tensor_sub(s[:, 3:4], s[:, 3:4], s[:, 7:8]),
                            lambda e: e.tensor_scalar(s[:, 3:4], s[:, 3:4], EPS,
                                                      None, ALU.add),
                        ])
                    plan.op("vector", fin, [wac], [(dv, 1)])
                    wdvf = dv.rec()

                    def rstd(eng, bi=bi):
                        s = s_st2[bi]
                        return chained(eng, CAC, [
                            lambda e: e.activation(s[:, 4:5], s[:, 3:4], AF.Ln),
                            lambda e: e.activation(s[:, 5:6], s[:, 4:5], AF.Exp,
                                                   scale=-0.5),
                        ])
                    plan.op("scalar", rstd, [wdvf], [(ac, 1)])
                    wacr = ac.rec()

                    def nbias(eng, bi=bi):
                        s = s_st2[bi]
                        return chained(eng, CDV, [
                            lambda e: e.tensor_mul(s[:, 6:7], s[:, 2:3], s[:, 5:6]),
                            lambda e: e.tensor_scalar(s[:, 6:7], s[:, 6:7], -1.0,
                                                      None, ALU.mult),
                        ])
                    plan.op("vector", nbias, [wacr], [(dv, 1)])
                    r_fin[t] = dv.rec()
                    wnorm = [r_fin[t]]
                    if t >= 2:
                        if dst_image is None:
                            wnorm.append(r_stl[t - 2])
                        else:
                            wnorm.append(r_trl[t - 2])
                    plan.op("scalar",
                            lambda eng, xb=xb, bi=bi: eng.activation(
                                s_xh[bi][:], s_x[xb][:], AF.Identity,
                                bias=s_st2[bi][:, 6:7], scale=s_st2[bi][:, 5:6]),
                            wnorm, [(ac, 1)])
                    r_norm[t] = ac.rec()
                    if dst_image is None:
                        r_stl[t] = ST.issue(
                            lambda eng, t=t, bi=bi: eng.dma_start(
                                out=xh_plain[t * 128:(t + 1) * 128, :], in_=s_xh[bi][:]),
                            [r_norm[t]])
                    else:
                        for grpi in range(4):
                            w2 = [r_norm[t]] if grpi == 0 else []
                            if t >= 1:
                                w2.append(r_evl[t - 1])
                            if t >= 2 and grpi == 0:
                                w2.append(r_stl[t - 2])
                            for j in range(8):
                                kc = grpi * 8 + j
                                inc = [(pe, 1)] if j == 7 else []
                                plan.op("tensor",
                                        lambda eng, bi=bi, kc=kc, grpi=grpi, j=j:
                                        eng.transpose(
                                            p_tr[grpi][:, j * 128:(j + 1) * 128],
                                            s_xh[bi][:, kc * 128:(kc + 1) * 128],
                                            c_idb[:]),
                                        w2 if j == 0 else (), inc)
                            tok = pe.rec()
                            if grpi == 3:
                                r_trl[t] = tok
                            plan.op("vector",
                                    lambda eng, bi=bi, grpi=grpi: eng.tensor_copy(
                                        s_wtile[bi][:, grpi * 1024:(grpi + 1) * 1024],
                                        p_tr[grpi][:, :]),
                                    [tok], [(dv, 1)])
                        r_evl[t] = dv.rec()
                        r_stl[t] = ST.issue(
                            lambda eng, t=t, bi=bi: eng.dma_start(
                                out=dst_image[t], in_=s_wtile[bi][:]),
                            [r_evl[t]])
                fence()

        def plan_ffn(xh_image, res_dram, out_dram, n_tiles, tag=""):
            with ExitStack() as ph:
                PE_ = ph.enter_context
                s_wth = PE_(nc.sbuf_tensor("ffn_wt" + tag, [128, NT * 2048], BF16))
                c_bh = PE_(nc.sbuf_tensor("ffn_bh" + tag, [1, D], BF16))
                c_bl = PE_(nc.sbuf_tensor("ffn_bl" + tag, [1, D], BF16))
                _b32s = ExitStack()
                c_b32 = _b32s.enter_context(nc.sbuf_tensor("ffn_b32" + tag, [1, D], F32))
                t_b = LD.issue(lambda eng: eng.dma_start(out=c_b32[:], in_=fbias))

                def _bsplit(eng):
                    return chained(eng, CDV, [
                        lambda e: e.tensor_copy(c_bh[:], c_b32[:]),
                        lambda e: e.tensor_sub(c_bl[:], c_b32[:], c_bh[:]),
                    ])
                plan.op("vector", _bsplit, [t_b], [(dv, 1)])
                _bias_tok = dv.rec()
                _b32s.close()
                s_xt = [PE_(nc.sbuf_tensor(f"ffn_xt{tag}{i}", [128, D], BF16))
                        for i in range(2)]
                s_res = [PE_(nc.sbuf_tensor(f"ffn_r{tag}{i}", [128, 2048], F32))
                         for i in range(2)]
                s_o = [PE_(nc.sbuf_tensor(f"ffn_o{tag}{i}", [128, 2048], F32))
                       for i in range(2)]
                p_acc = [PE_(nc.psum_tensor(f"ffn_p{tag}{i}", [128, 2048], F32))
                         for i in range(2)]

                r_mm = {}
                r_epi = {}
                r_sto = {}
                for half in range(2):
                    ww = [r_mm[n_tiles - 1]] if half == 1 else []
                    wt_toks = []
                    for part in range(4):
                        src = d_wt.rearrange("(kc p) n -> p kc n", p=128)[
                            :, part * 8:(part + 1) * 8, half * 2048:(half + 1) * 2048]
                        dstv = s_wth[:].rearrange("p (kc n) -> p kc n", n=2048)[
                            :, part * 8:(part + 1) * 8, :]
                        wt_toks.append(LD.issue(
                            lambda eng, src=src, dstv=dstv: eng.dma_start(
                                out=dstv, in_=src),
                            ww if part == 0 else []))
                    for t in range(n_tiles):
                        gi = half * n_tiles + t
                        bi = gi % 2
                        xb = gi % 2
                        wl = list(wt_toks) if t == 0 else []
                        if gi == 0:
                            wl.append(_bias_tok)
                        if gi >= 2:
                            wl.append(r_mm[gi - 2])
                        t_xt = LD.issue(
                            lambda eng, t=t, xb=xb: eng.dma_start(
                                out=s_xt[xb][:], in_=xh_image[t]),
                            wl)
                        wl2 = [r_epi[gi - 2]] if gi >= 2 else []
                        if gi <= 1:
                            wl2.append(_bias_tok)
                        t_res = LD.issue(
                            lambda eng, t=t, xb=xb, half=half: eng.dma_start(
                                out=s_res[xb][:],
                                in_=res_dram[t * 128:(t + 1) * 128,
                                             half * 2048:(half + 1) * 2048]),
                            wl2)
                        wseed = [t_xt, (gp, 3), _bias_tok]
                        if gi >= 2:
                            wseed.append(r_epi[gi - 2])

                        def seeds(eng, bi=bi, half=half):
                            last = None
                            for nb4 in range(4):
                                cols = slice(half * 2048 + nb4 * 512,
                                             half * 2048 + nb4 * 512 + 512)
                                pcol = slice(nb4 * 512, nb4 * 512 + 512)
                                eng.matmul(p_acc[bi][:, pcol], c_ones1[:], c_bh[:, cols],
                                           start=True, stop=False)
                                last = eng.matmul(p_acc[bi][:, pcol], c_ones1[:],
                                                  c_bl[:, cols], start=False, stop=False)
                            return last
                        plan.op("tensor", seeds, wseed, ())

                        def mms(eng, bi=bi, xb=xb):
                            last = None
                            for kc in range(32):
                                for nb4 in range(4):
                                    pcol = slice(nb4 * 512, nb4 * 512 + 512)
                                    wcol = slice(kc * 2048 + nb4 * 512,
                                                 kc * 2048 + nb4 * 512 + 512)
                                    last = eng.matmul(
                                        p_acc[bi][:, pcol],
                                        s_xt[xb][:, kc * 128:(kc + 1) * 128],
                                        s_wth[:, wcol],
                                        start=False, stop=(kc == 31))
                            return last
                        plan.op("tensor", mms, [], [(pe, 1)])
                        r_mm[gi] = pe.rec()
                        wep = [r_mm[gi], t_res]
                        if gi >= 2:
                            wep.append(r_sto[gi - 2])
                        plan.op("vector",
                                lambda eng, bi=bi, xb=xb: eng.scalar_tensor_tensor(
                                    s_o[bi][:], p_acc[bi][:], 0.5, s_res[xb][:],
                                    ALU.mult, ALU.add),
                                wep, [(dv, 1)])
                        r_epi[gi] = dv.rec()
                        r_sto[gi] = ST.issue(
                            lambda eng, t=t, bi=bi, half=half: eng.dma_start(
                                out=out_dram[t * 128:(t + 1) * 128,
                                             half * 2048:(half + 1) * 2048],
                                in_=s_o[bi][:]),
                            [r_epi[gi]])
                fence()

        plan_ln(x_in, NT, d_x1h, tag="a")
        plan_ffn(d_x1h, x_in, d_x2, NT, tag="a")

        plan_ln(d_x2, NT, None, xh_plain=d_x2h, tag="b")

        with ExitStack() as ph:
            PE_ = ph.enter_context
            s_pan = PE_(nc.sbuf_tensor("p2_pan", [128, 32 * 2048], BF16))
            s_wb = [PE_(nc.sbuf_tensor(f"p2_wb{i}", [128, 32 * 128], BF16))
                    for i in range(2)]
            s_prj = [PE_(nc.sbuf_tensor(f"p2_prj{i}", [128, 2048], BF16))
                     for i in range(2)]
            s_t1 = PE_(nc.sbuf_tensor("p2_t1", [64, 1024], F32))
            s_t2 = PE_(nc.sbuf_tensor("p2_t2", [64, 1024], F32))
            s_csA = PE_(nc.sbuf_tensor("p2_csA", [128, 2048], F32))
            s_csB = PE_(nc.sbuf_tensor("p2_csB", [128, 2048], F32))
            s_pcp = PE_(nc.sbuf_tensor("p2_pcp", [128, 2048], F32))
            _p2p = PE_(nc.psum_tensor("p2_p0", [128, 2048], F32))
            p_acc = [_p2p, _p2p]

            r_mm2 = {}
            r_pcons = {}
            r_scons = {}
            r_ev2 = {}
            prev_rope_tok = None
            for th in range(2):
                chunks = [("k", i) for i in range(HK)] + [("v", i) for i in range(HK)]
                if th == 0:
                    chunks += [("q", i) for i in range(HQ)]
                wcs = []
                if th == 1:
                    wcs = [prev_rope_tok]
                cs_toks = [
                    LD.issue(lambda eng, th=th: eng.dma_start(
                        out=s_csA[0:64, :], in_=cosr[:, th * 2048:(th + 1) * 2048]), wcs),
                    LD.issue(lambda eng, th=th: eng.dma_start(
                        out=s_csA[64:128, :], in_=sinr[:, th * 2048:(th + 1) * 2048])),
                    LD.issue(lambda eng, th=th: eng.dma_start(
                        out=s_csB[0:64, :], in_=sinr[:, th * 2048:(th + 1) * 2048])),
                    LD.issue(lambda eng, th=th: eng.dma_start(
                        out=s_csB[64:128, :], in_=cosr[:, th * 2048:(th + 1) * 2048])),
                ]
                pan_toks = []
                for kc in range(32):
                    w = []
                    if th == 1 and kc == 0:
                        w = [r_mm2[(0, 47)]]
                    pan_toks.append(LD.issue(
                        lambda eng, kc=kc, th=th: eng.dma_start(
                            out=s_pan[:, kc * 2048:(kc + 1) * 2048],
                            in_=d_x2h[th * 2048:(th + 1) * 2048,
                                      kc * 128:(kc + 1) * 128],
                            transpose=True),
                        w))
                for ci, (kind, hh) in enumerate(chunks):
                    bi = ci % 2
                    wsrc = {"k": wk, "v": wv, "q": wq}[kind]
                    w = [r_mm2[(th, ci - 2)]] if ci >= 2 else (
                        [r_mm2[(0, 47)]] if th == 1 else [])
                    wb_toks = []
                    for part in range(4):
                        src = wsrc.rearrange("(kc p) n -> p kc n", p=128)[
                            :, part * 8:(part + 1) * 8, hh * 128:(hh + 1) * 128]
                        dstv = s_wb[bi][:].rearrange("p (kc n) -> p kc n", n=128)[
                            :, part * 8:(part + 1) * 8, :]
                        wb_toks.append(GD.issue(
                            lambda eng, src=src, dstv=dstv: eng.dma_start(
                                out=dstv, in_=src),
                            w if part == 0 else []))
                    wmm = wb_toks + pan_toks
                    if ci >= 1:
                        wmm.append(r_pcons[(th, ci - 1)])
                    elif th == 1:
                        wmm.append(r_pcons[(0, 47)])

                    def mms(eng, bi=bi):
                        last = None
                        for kc in range(32):
                            for nb4 in range(4):
                                pcol = slice(nb4 * 512, nb4 * 512 + 512)
                                last = eng.matmul(
                                    p_acc[bi][:, pcol],
                                    s_wb[bi][:, kc * 128:(kc + 1) * 128],
                                    s_pan[:, kc * 2048 + nb4 * 512:
                                          kc * 2048 + nb4 * 512 + 512],
                                    start=(kc == 0), stop=(kc == 31))
                        return last
                    plan.op("tensor", mms, wmm, [(pe, 1)])
                    r_mm2[(th, ci)] = pe.rec()

                    if kind in ("k", "q"):
                        plan.op("vector",
                                lambda eng, bi=bi: eng.tensor_copy(
                                    s_pcp[:], p_acc[bi][:]),
                                [r_mm2[(th, ci)]], [(dv, 1)])
                        r_pcons[(th, ci)] = dv.rec()

                        def rope(eng, bi=bi):
                            pa = s_pcp
                            steps = []
                            for q2 in range(2):
                                cs = slice(q2 * 1024, (q2 + 1) * 1024)
                                steps += [
                                    lambda e, cs=cs: e.tensor_mul(
                                        s_t1[:], pa[0:64, cs], s_csA[0:64, cs]),
                                    lambda e, cs=cs: e.tensor_mul(
                                        s_t2[:], pa[64:128, cs], s_csA[64:128, cs]),
                                    lambda e, cs=cs, bi=bi: e.tensor_sub(
                                        s_prj[bi][0:64, cs], s_t1[:], s_t2[:]),
                                    lambda e, cs=cs: e.tensor_mul(
                                        s_t1[:], pa[0:64, cs], s_csB[0:64, cs]),
                                    lambda e, cs=cs: e.tensor_mul(
                                        s_t2[:], pa[64:128, cs], s_csB[64:128, cs]),
                                    lambda e, cs=cs, bi=bi: e.tensor_add(
                                        s_prj[bi][64:128, cs], s_t1[:], s_t2[:]),
                                ]
                            return chained(eng, CDV, steps)
                        wr = list(cs_toks)
                        if ci >= 2:
                            wr.append(r_scons[(th, ci - 2)])
                        elif th == 1:
                            wr.append(r_scons[(0, 46 + ci)])
                        plan.op("vector", rope, wr, [(dv, 1)])
                        tokr = dv.rec()
                        prev_rope_tok = tokr
                        dst = d_k[hh, :, th * 2048:(th + 1) * 2048] if kind == "k" \
                            else d_q[hh]
                        r_scons[(th, ci)] = ST.issue(
                            lambda eng, dst=dst, bi=bi: eng.dma_start(
                                out=dst, in_=s_prj[bi][:]),
                            [tokr])
                    else:
                        wr = [r_mm2[(th, ci)]]
                        if ci >= 2:
                            wr.append(r_scons[(th, ci - 2)])
                        elif th == 1:
                            wr.append(r_scons[(0, 46 + ci)])
                        plan.op("vector",
                                lambda eng, bi=bi: eng.tensor_copy(
                                    s_prj[bi][:], p_acc[bi][:]),
                                wr, [(dv, 1)])
                        wev = dv.rec()
                        r_pcons[(th, ci)] = wev
                        dst = d_v[hh][:, th * 2048:(th + 1) * 2048]
                        r_scons[(th, ci)] = ST.issue(
                            lambda eng, dst=dst, bi=bi: eng.dma_start(
                                out=dst, in_=s_prj[bi][:]),
                            [wev])
            fence()

        with ExitStack() as ph:
            PE_ = ph.enter_context
            s_k = [PE_(nc.sbuf_tensor(f"a_k{i}", [128, TLOC], BF16)) for i in range(2)]
            s_v = [PE_(nc.sbuf_tensor(f"a_v{i}", [128, NT * 128], BF16))
                   for i in range(2)]
            s_q = [PE_(nc.sbuf_tensor(f"a_q{i}", [128, 4 * TOWN], BF16))
                   for i in range(2)]
            s_pt = PE_(nc.sbuf_tensor("a_pt", [128, 8 * 512], BF16))
            s_m1 = [PE_(nc.sbuf_tensor(f"a_m1{i}", [128, 64], F32)) for i in range(2)]
            s_mcol = [PE_(nc.sbuf_tensor(f"a_mcol{i}", [128, 1], F32)) for i in range(2)]
            s_bias = [PE_(nc.sbuf_tensor(f"a_bias{i}", [128, 1], F32)) for i in range(2)]
            s_mred = [PE_(nc.sbuf_tensor(f"a_mred{i}", [1, 1], F32)) for i in range(2)]
            s_sumr = PE_(nc.sbuf_tensor("a_sumr", [1, 512], F32))
            s_rb = PE_(nc.sbuf_tensor("a_rb", [128, 512], F32))
            s_ot = [PE_(nc.sbuf_tensor(f"a_ot{i}", [128, 512], BF16))
                    for i in range(2)]
            RING = 40
            s_ring = PE_(nc.sbuf_tensor("a_ring", [128, RING * 512], F32))
            p_s2 = [PE_(nc.psum_tensor(f"a_s2{i}", [128, 512], F32)) for i in range(3)]
            p_o = [PE_(nc.psum_tensor(f"a_o{i}", [128, 512], F32)) for i in range(2)]
            p_sum = [PE_(nc.psum_tensor(f"a_sum{i}", [128, 512], F32))
                     for i in range(2)]
            p_scr = PE_(nc.psum_tensor("a_scr", [128, 512], F32))

            def ktiles(qb):
                out = []
                for sp in range(4 * qb + 4):
                    for side in range(2):
                        col = sp + 16 * side
                        delta = sp - 4 * qb
                        r0 = max(0, delta) * 128
                        msk = None
                        if delta >= 0:
                            msk = c_mtri if side == 0 else c_mpar
                        out.append((col, r0, msk))
                return out

            LOOKAHEAD = 6
            items = []
            for g in range(HK):
                for hh in range(4):
                    for qb in range(4):
                        items.append((g, hh, qb))
            NITEM = len(items)

            g_toks = {}
            r_gend = {}
            r_copy = {}
            r_mskd = {}
            r_exp = {}
            r_avt = {}
            r_ptfree = {}
            it_state = {}
            kg1 = [0]
            kg2 = [0]
            r_evo = {}
            r_evoc = {}
            r_fins = {}
            r_store = {}
            tails = {}

            def emit_loads(g):
                w = [r_gend[g - 2]] if g >= 2 else []
                toks = [LD.issue(lambda eng, g=g: eng.dma_start(
                    out=s_k[g % 2][:], in_=d_k[g]), w)]
                for vt in range(NT):
                    toks.append(LD.issue(
                        lambda eng, g=g, vt=vt: eng.dma_start(
                            out=s_v[g % 2][:, vt * 128:(vt + 1) * 128],
                            in_=d_v[g][:, vt * 128:(vt + 1) * 128],
                            transpose=True)))
                for hh in range(4):
                    toks.append(LD.issue(
                        lambda eng, g=g, hh=hh: eng.dma_start(
                            out=s_q[g % 2][:, hh * TOWN:(hh + 1) * TOWN],
                            in_=d_q[4 * g + hh])))
                g_toks[g] = toks

            def gen_p1(m):
                g, hh, qb = items[m]
                gb = g % 2
                mb = m % 2
                kts = ktiles(qb)
                it_state[m] = {"kts": kts}
                first_of_g = (hh == 0 and qb == 0)
                lastred = None
                for i, (col, r0, msk) in enumerate(kts):
                    kg = kg1[0]
                    sb = kg % 3
                    slot = kg % RING
                    w1 = []
                    if first_of_g and i == 0:
                        w1 += g_toks[g]
                    if kg >= 3:
                        w1.append(r_copy[kg - 3])
                    plan.op("tensor",
                            lambda eng, gb=gb, col=col, sb=sb, hh=hh, qb=qb:
                            eng.matmul(
                                p_s2[sb][:, 0:512],
                                s_k[gb][:, col * 128:(col + 1) * 128],
                                s_q[gb][:, hh * TOWN + qb * 512:
                                        hh * TOWN + qb * 512 + 512],
                                start=True, stop=True),
                            w1, [(pe, 1)])
                    tokm = pe.rec()
                    rc = 0 if kg < RING else r0
                    wc = [tokm]
                    if kg >= RING:
                        wc.append(r_exp[kg - RING])
                    plan.op("scalar",
                            lambda eng, sb=sb, slot=slot, rc=rc: eng.activation(
                                s_ring[:, slot * 512 + rc:slot * 512 + 512],
                                p_s2[sb][:, rc:512], AF.Identity),
                            wc, [(ac, 1)])
                    r_copy[kg] = ac.rec()
                    if msk is not None:
                        plan.op("vector",
                                lambda eng, slot=slot, r0=r0, msk=msk:
                                eng.tensor_add(
                                    s_ring[:, slot * 512 + r0:
                                           slot * 512 + r0 + 128],
                                    s_ring[:, slot * 512 + r0:
                                           slot * 512 + r0 + 128], msk[:]),
                                [r_copy[kg]], [(dv, 1)])
                        r_mskd[kg] = dv.rec()
                    if i % 4 == 3:
                        pslot = (kg - 3) % RING
                        plan.op("vector",
                                lambda eng, i=i, mb=mb, pslot=pslot:
                                eng.tensor_reduce(
                                    s_m1[mb][:, i // 4:i // 4 + 1],
                                    s_ring[:, pslot * 512:pslot * 512 + 2048],
                                    AX.X, ALU.max),
                                [r_copy[kg]], [(dv, 1)])
                        lastred = dv.rec()
                    kg1[0] += 1
                    yield
                nk = len(kts)
                plan.op("vector",
                        lambda eng, nk=nk, mb=mb: eng.tensor_reduce(
                            s_mcol[mb][:], s_m1[mb][:, 0:nk // 4], AX.X, ALU.max),
                        [lastred], [(dv, 1)])
                it_state[m]["comb"] = dv.rec()

            def emit_chain(m):
                st_ = it_state[m]
                mb = m % 2
                wtr = [st_["comb"]]
                if m - 2 in r_fins:
                    wtr.append(r_fins[m - 2])
                plan.op("tensor",
                        lambda eng, mb=mb: eng.transpose(
                            p_sum[mb][0:1, 0:128], s_mcol[mb][:], c_idf[:]),
                        wtr, [(pe, 1)])
                tok2 = pe.rec()
                plan.op("vector",
                        lambda eng, mb=mb: eng.tensor_reduce(
                            s_mred[mb][:], p_sum[mb][0:1, 0:128], AX.X, ALU.max),
                        [tok2], [(dv, 1)])
                tok3 = dv.rec()
                plan.op("tensor",
                        lambda eng, mb=mb: eng.matmul(
                            p_sum[mb][:, 0:1], c_onesf[:], s_mred[mb][:],
                            start=True, stop=True),
                        [tok3], [(pe, 1)])
                tok4 = pe.rec()

                def mkbias(eng, mb=mb):
                    return chained(eng, CDV, [
                        lambda e: e.tensor_copy(s_mcol[mb][:],
                                                p_sum[mb][:, 0:1]),
                        lambda e: e.tensor_scalar(s_bias[mb][:], s_mcol[mb][:],
                                                  -SCL, None, ALU.mult),
                    ])
                plan.op("vector", mkbias, [tok4], [(dv, 1)])
                st_["bias"] = dv.rec()

            def tail_gen(m, wav, minr0):
                g, hh, qb = items[m]
                mb = m % 2
                order = [j for j in range(3) if minr0[j] < 512]
                plan.op("vector", lambda eng: eng.memset(s_sumr[:], 0.0),
                        [r_evoc[m - 1]] if m - 1 in r_evoc else [], [(dv, 1)])
                yield
                for ix, j in enumerate(order):
                    w = [wav] if ix == 0 else []
                    plan.op("vector",
                            lambda eng, j=j, a=minr0[j], mb=mb:
                            eng.scalar_tensor_tensor(
                                s_sumr[:, a:512],
                                p_sum[mb][32 * j:32 * j + 1, a:512],
                                1.0, s_sumr[:, a:512], ALU.mult, ALU.add),
                            w, [(dv, 1)])
                    yield
                r_fins[m] = dv.rec()
                plan.op("vector",
                        lambda eng: eng.tensor_scalar(
                            s_sumr[:], s_sumr[:], 1e-30, None, ALU.add),
                        [], [(dv, 1)])
                yield
                plan.op("vector",
                        lambda eng: eng.reciprocal(s_sumr[:], s_sumr[:]),
                        [], [(dv, 1)])
                tokr = dv.rec()
                yield
                wrb = [tokr]
                if m - 1 in r_evoc:
                    wrb.append(r_evoc[m - 1])
                plan.op("tensor",
                        lambda eng: eng.matmul(p_scr[:], c_onesf[:], s_sumr[:],
                                               start=True, stop=True),
                        wrb, [(pe, 1)])
                tok6 = pe.rec()
                yield
                plan.op("vector",
                        lambda eng: eng.tensor_copy(s_rb[:], p_scr[:]),
                        [tok6], [(dv, 1)])
                r_evoc[m] = dv.rec()
                yield
                wev = []
                if m - 2 in r_store:
                    wev.append(r_store[m - 2])
                plan.op("vector",
                        lambda eng, mb=mb: eng.tensor_mul(
                            s_ot[mb][:], p_o[mb][:], s_rb[:]),
                        wev, [(dv, 1)])
                r_evo[m] = dv.rec()
                yield
                dst = d_ot[(4 * g + hh) * 128:(4 * g + hh + 1) * 128,
                           qb * 512:(qb + 1) * 512]
                r_store[m] = ST.issue(
                    lambda eng, dst=dst, mb=mb: eng.dma_start(
                        out=dst, in_=s_ot[mb][:]),
                    [r_evo[m]])
                yield

            def gen_p2(m):
                g, hh, qb = items[m]
                gb = g % 2
                mb = m % 2
                kts = it_state[m]["kts"]
                nk = len(kts)
                pend = []
                avq = []
                prev_ent = [None]
                first_grp = [True]
                tg = tails.get(m - 1)

                def emit_av(ent, i_now):
                    ptslot, r0, kgx, li, col, last = ent
                    first = (li == 0)

                    def av(eng, gb=gb, col=col, r0=r0, ptslot=ptslot,
                           first=first, last=last, mb=mb):
                        return eng.matmul(p_o[mb][:, r0:512],
                                          s_v[gb][:, col * 128:(col + 1) * 128],
                                          s_pt[:, ptslot * 512 + r0:
                                               ptslot * 512 + 512],
                                          start=first, stop=last)
                    wav_ = [r_exp[kgx]]
                    if first and m - 2 in r_evo:
                        wav_.append(r_evo[m - 2])
                    plan.op("tensor", av, wav_, [(pe, 1)])
                    r_avt[kgx] = pe.rec()
                    pend.append((ptslot, r0, kgx, li))
                    if len(pend) == 3 or last:
                        def sums(eng, pend=tuple(pend), nk=nk, mb=mb):
                            lastins = None
                            for (sb_, r0_, _, lj) in pend:
                                j = lj % 3
                                lastins = eng.matmul(
                                    p_sum[mb][32 * j:32 * j + 1, r0_:512],
                                    c_onescol[:],
                                    s_pt[:, sb_ * 512 + r0_:sb_ * 512 + 512],
                                    start=(lj < 3), stop=(lj + 3 >= nk),
                                    tile_position=(0, 32 * j))
                            return lastins
                        wsum = [r_exp[pend[-1][2]]]
                        if first_grp[0]:
                            first_grp[0] = False
                            if m - 2 in r_fins:
                                wsum.append(r_fins[m - 2])
                        plan.op("tensor", sums, wsum, [(pe, 1)])
                        grp_tok = pe.rec()
                        for (_, _, kgx2, _) in pend:
                            r_ptfree[kgx2] = grp_tok
                        del pend[:]

                for i, (col, r0, msk) in enumerate(kts):
                    kg = kg2[0]
                    slot = kg % RING
                    ptslot = kg % 8
                    if i % 2 == 1:
                        eslot = (kg - 1) % RING
                        pslot = (kg - 1) % 8
                        wexp = []
                        if i == 1:
                            wexp.append(it_state[m]["bias"])
                        for kk in (kg - 1, kg):
                            if kk in r_mskd:
                                wexp.append(r_mskd[kk])
                        if kg >= 9:
                            wexp.append(r_ptfree[kg - 9])
                        if kg >= 8:
                            wexp.append(r_ptfree[kg - 8])
                        plan.op("scalar",
                                lambda eng, eslot=eslot, pslot=pslot, r0=r0,
                                mb=mb: eng.activation(
                                    s_pt[:, pslot * 512 + r0:pslot * 512 + 1024],
                                    s_ring[:, eslot * 512 + r0:
                                           eslot * 512 + 1024],
                                    AF.Exp, bias=s_bias[mb][:], scale=SCL),
                                wexp, [(ac, 1)])
                        r_exp[kg - 1] = r_exp[kg] = ac.rec()
                        avq.append(prev_ent[0])
                        avq.append((ptslot, r0, kg, i, col, i == nk - 1))
                        while len(avq) > 3:
                            emit_av(avq.pop(0), i)
                    else:
                        prev_ent[0] = (ptslot, r0, kg, i, col, i == nk - 1)
                    if tg is not None:
                        next(tg, None)
                    kg2[0] += 1
                    yield
                while avq:
                    emit_av(avq.pop(0), nk - 1)
                if tg is not None:
                    for _ in tg:
                        pass
                    tails.pop(m - 1, None)
                wav = r_ptfree[kg2[0] - 1]
                minr0 = [512, 512, 512]
                for li, (_, r0_, _) in enumerate(kts):
                    j = li % 3
                    minr0[j] = min(minr0[j], r0_)
                tails[m] = tail_gen(m, wav, minr0)
                if hh == 3 and qb == 3:
                    r_gend[g] = pe.rec()
                yield

            def drain(gen):
                for _ in gen:
                    pass

            emit_loads(0)
            emit_loads(1)
            drain(gen_p1(0))
            for m in range(NITEM):
                g, hh, qb = items[m]
                if hh == 0 and qb == 1 and (g + 1) < HK and (g + 1) not in g_toks:
                    emit_loads(g + 1)
                A = gen_p1(m + 1) if m + 1 < NITEM else iter(())
                for _ in range(LOOKAHEAD):
                    if next(A, "end") == "end":
                        break
                emit_chain(m)
                B = gen_p2(m)
                alive = True
                while alive:
                    alive = False
                    if next(A, "end") != "end":
                        alive = True
                    if next(B, "end") != "end":
                        alive = True
            for tg in tails.values():
                for _ in tg:
                    pass
            fence()

        with ExitStack() as ph:
            PE_ = ph.enter_context
            s_wo = PE_(nc.sbuf_tensor("w_wo", [128, 32 * 2048], BF16))
            s_otl = [PE_(nc.sbuf_tensor(f"w_ot{i}", [128, D], BF16)) for i in range(2)]
            s_res = [PE_(nc.sbuf_tensor(f"w_r{i}", [128, 2048], F32)) for i in range(2)]
            s_o = [PE_(nc.sbuf_tensor(f"w_o{i}", [128, 2048], F32)) for i in range(2)]
            p_acc = [PE_(nc.psum_tensor(f"w_p{i}", [128, 2048], F32)) for i in range(2)]

            r_mm5 = {}
            r_epi5 = {}
            r_sto5 = {}
            for half in range(2):
                ww = [r_mm5[NTOWN - 1]] if half == 1 else []
                wo_toks = []
                for cc in range(32):
                    src = wo.rearrange("(cc p) n -> cc p n", p=128)[
                        cc, :, half * 2048:(half + 1) * 2048]
                    wo_toks.append(GD.issue(
                        lambda eng, src=src, cc=cc: eng.dma_start(
                            out=s_wo[:, cc * 2048:(cc + 1) * 2048], in_=src),
                        ww if cc == 0 else []))
                for t in range(NTOWN):
                    gi = half * NTOWN + t
                    bi = gi % 2
                    xb = gi % 2
                    wl = list(wo_toks) if t == 0 else []
                    if gi >= 2:
                        wl.append(r_mm5[gi - 2])
                    ot_toks = []
                    for part in range(4):
                        src = d_ot.rearrange("(cc p) (t n) -> cc p t n", p=128, n=128)[
                            part * 8:(part + 1) * 8, :, t, :].rearrange(
                            "cc p n -> p cc n")
                        dstv = s_otl[xb][:].rearrange("p (cc n) -> p cc n", n=128)[
                            :, part * 8:(part + 1) * 8, :]
                        ot_toks.append(LD.issue(
                            lambda eng, src=src, dstv=dstv: eng.dma_start(
                                out=dstv, in_=src),
                            wl if part == 0 else []))
                    wl2 = [r_epi5[gi - 2]] if gi >= 2 else []
                    t_res = LD.issue(
                        lambda eng, t=t, xb=xb, half=half: eng.dma_start(
                            out=s_res[xb][:],
                            in_=d_x2[t * 128:(t + 1) * 128,
                                     half * 2048:(half + 1) * 2048]),
                        wl2)
                    wmm = list(ot_toks)
                    if gi >= 2:
                        wmm.append(r_epi5[gi - 2])

                    def mms(eng, bi=bi, xb=xb):
                        last = None
                        for cc in range(32):
                            for nb4 in range(4):
                                pcol = slice(nb4 * 512, nb4 * 512 + 512)
                                wcol = slice(cc * 2048 + nb4 * 512,
                                             cc * 2048 + nb4 * 512 + 512)
                                last = eng.matmul(
                                    p_acc[bi][:, pcol],
                                    s_otl[xb][:, cc * 128:(cc + 1) * 128],
                                    s_wo[:, wcol],
                                    start=(cc == 0), stop=(cc == 31))
                        return last
                    plan.op("tensor", mms, wmm, [(pe, 1)])
                    r_mm5[gi] = pe.rec()
                    wep = [r_mm5[gi], t_res]
                    if gi >= 2:
                        wep.append(r_sto5[gi - 2])
                    plan.op("vector",
                            lambda eng, bi=bi, xb=xb: eng.tensor_add(
                                s_o[bi][:], p_acc[bi][:], s_res[xb][:]),
                            wep, [(dv, 1)])
                    r_epi5[gi] = dv.rec()
                    r_sto5[gi] = ST.issue(
                        lambda eng, t=t, bi=bi, half=half: eng.dma_start(
                            out=d_x3[t * 128:(t + 1) * 128,
                                     half * 2048:(half + 1) * 2048],
                            in_=s_o[bi][:]),
                        [r_epi5[gi]])
            fence()

        plan_ln(d_x3, NTOWN, d_x3h, tag="c")
        plan_ffn(d_x3h, d_x3, o_out, NTOWN, tag="c")

        plan.wait_all("sync", SEMS)

        with nc.Block() as blk:
            @blk.sync
            def _(eng):
                for f in plan.q["sync"]:
                    f(eng)

            @blk.gpsimd
            def _(eng):
                for f in plan.q["gpsimd"]:
                    f(eng)

            @blk.tensor
            def _(eng):
                for f in plan.q["tensor"]:
                    f(eng)

            @blk.vector
            def _(eng):
                for f in plan.q["vector"]:
                    f(eng)

            @blk.scalar
            def _(eng):
                for f in plan.q["scalar"]:
                    f(eng)

    return nc


def _host_inputs(inputs):
    X = np.asarray(inputs["L0_attn_X"])
    Wq = np.ascontiguousarray(np.asarray(inputs["L0_attn_Wq"]))
    Wk = np.ascontiguousarray(np.asarray(inputs["L0_attn_Wk"]))
    Wv = np.ascontiguousarray(np.asarray(inputs["L0_attn_Wv"]))
    Wo = np.ascontiguousarray(np.asarray(inputs["L0_attn_Wo"]))
    Wqi = np.ascontiguousarray(np.asarray(inputs["L0_ffn_weight_q"]))
    Zp = np.ascontiguousarray(np.asarray(inputs["L0_ffn_zp"]))
    Sc = np.ascontiguousarray(np.asarray(inputs["L0_ffn_scale"]))
    Bias = np.ascontiguousarray(np.asarray(inputs["L0_ffn_bias"]).reshape(1, D))
    ident = np.eye(128, dtype=np.float32)

    inv_freq = (1.0 / (10000.0 ** (np.arange(0, 64, dtype=np.float32) / 64.0))) \
        .astype(np.float32)
    tri = np.where(np.arange(128)[:, None] <= np.arange(128)[None, :], 0.0, NEG) \
        .astype(np.float32)

    in_maps = []
    perms = []
    for c in range(8):
        b, p = c // 2, c % 2
        own = np.concatenate([np.arange(128 * (2 * s + p), 128 * (2 * s + p) + 128)
                              for s in range(16)])
        oth = np.concatenate([np.arange(128 * (2 * s + 1 - p),
                                        128 * (2 * s + 1 - p) + 128)
                              for s in range(16)])
        perm = np.concatenate([own, oth])
        perms.append(own)
        pos = perm.astype(np.float32)
        ang = pos[None, :] * inv_freq[:, None]
        cosr_ = np.cos(ang).astype(np.float32)
        sinr_ = np.sin(ang).astype(np.float32)
        mpar = np.zeros((128, 128), np.float32) if p == 1 \
            else np.full((128, 128), NEG, np.float32)
        in_maps.append(dict(
            x_in=np.ascontiguousarray(X[b][perm]),
            wq=Wq, wk=Wk, wv=Wv, wo=Wo, wqint=Wqi, zp=Zp, qsc=Sc, fbias=Bias,
            cosr=cosr_, sinr=sinr_, mtri=tri, mpar=mpar, identf=ident,
        ))
    return in_maps, perms


def kernel(**inputs):
    in_maps, perms = _host_inputs(inputs)
    nc = build_nc()
    res = run_bass_kernel_spmd(nc, in_maps, core_ids=list(range(8)))
    out = np.empty((4, 4096, 4096), dtype=np.float32)
    for c in range(8):
        b = c // 2
        out[b][perms[c]] = res.results[c]["o_out"]
    return out


if __name__ == "__main__":
    import reference as _ref
    ins = _ref.setup_inputs(0)
    got = kernel(**{k: np.asarray(v) for k, v in ins.items()})
    exp = np.asarray(_ref.reference(**ins))
    num = float(np.mean((got - exp) ** 2))
    den = float(np.mean(exp ** 2))
    print("resvar:", num / den)
